# version 02
# baseline (speedup 1.0000x reference)
import numpy as np
import ml_dtypes
import concourse.bass as bass
import concourse.mybir as mybir
from concourse.bass_utils import run_bass_kernel_spmd

F32 = mybir.dt.float32
BF16 = mybir.dt.bfloat16
AF = mybir.ActivationFunctionType
ALU = mybir.AluOpType
AX = mybir.AxisListType
bfloat16 = ml_dtypes.bfloat16

B, L, D = 2, 4096, 4096
T = 1024
NT = T // 128
H, P64, NST, G, KCONV, CHUNK = 128, 64, 128, 8, 4, 128
INTER, CONV, PROJ = 8192, 10240, 18560
NCH = T // CHUNK
EPS = 1e-5
SBUF_TOP = 229376
NCORES = 8
STATESZ = H * P64 * NST

DRAM_REG = {
    "x":      ([T, D], F32),
    "xh":     ([3, D], F32),
    "wprojT": ([D, PROJ], F32),
    "cw":     ([128, 320], F32),
    "cb":     ([128, 80], F32),
    "dtb":    ([H, 1], F32),
    "alog":   ([H, 1], F32),
    "dd":     ([H, 1], F32),
    "nw":     ([128, 64], F32),
    "w2T":    ([INTER, D], F32),
    "fwT":    ([D, D], F32),
    "fbias":  ([1, D], F32),
    "osd":    ([T, D], BF16),
    "gwT":    ([D, 8], F32),
    "ics":    ([8, D], F32),
    "ocs":    ([8, D], F32),
    "wq":     ([D, D], F32),
    "wk":     ([D, 1024], F32),
    "wv":     ([D, 1024], F32),
    "wo":     ([1024, D], F32),
    "bwT":    ([D, D], F32),
    "bb":     ([1, D], F32),
    "att":    ([D, 64], F32),
    "bt2":    ([1024, D], F32),
    "rwT":    ([D, 16], F32),
    "idt":    ([128, 128], BF16),
    "oner":   ([1, 512], BF16),
    "mask01": ([128, 128], F32),
    "coef":   ([1, 64], F32),
    "tln0":   ([T, D], BF16),
    "tlnh":   ([3, D], BF16),
    "gatefm": ([INTER, T], BF16),
    "hsfm":   ([INTER, T], BF16),
    "bcfm":   ([2048, T], BF16),
    "dtac":   ([H, T], F32),
    "acsr":   ([NCH, H * CHUNK], F32),
    "saux":   ([5 * 128, T], F32),
    "sleft":  ([128, INTER], F32),
    "coefc":  ([128, 40], F32),
    "idtf":   ([128, 128], F32),
    "ddr":    ([1, H], F32),
    "maskneg": ([128, 128], F32),
    "contrib": ([1, STATESZ + H], F32),
    "gath":   ([4, STATESZ + H], F32),
    "norm":   ([T, INTER], BF16),
    "abuf":   ([T, D], BF16),
    "res1":   ([T, D], F32),
    "tln1":   ([T, D], BF16),
    "fbuf":   ([T, D], BF16),
    "res2":   ([T, D], F32),
    "tln2":   ([T, D], BF16),
    "wqed":   ([D, 1024], BF16),
    "qt":     ([1024, T], BF16),
    "kcon":   ([1024, T], BF16),
    "vcon":   ([T, 1024], BF16),
    "kconI":  ([1024, T], BF16),
    "vconI":  ([T, 1024], BF16),
    "kg":     ([4 * 1024, T], BF16),
    "vg":     ([4 * T, 1024], BF16),
    "atto":   ([T, D], BF16),
    "res3":   ([T, D], F32),
    "tln3":   ([T, D], BF16),
    "out":    ([T, D], F32),
}


def cdiv(a, b):
    return (a + b - 1) // b


class PingDma:

    def __init__(self, pg, name, per=1):
        self.s = [pg.sem(name + "0"), pg.sem(name + "1")]
        self.per = per

    def inc(self, ins, idx):
        ins.then_inc(self.s[idx % 2], 16)

    def wait(self, e, idx):
        e.wait_ge(self.s[idx % 2], 16 * self.per * (idx // 2 + 1))

    def wait_all(self, e, n):
        if n >= 1:
            e.wait_ge(self.s[0], 16 * self.per * ((n + 1) // 2))
        if n >= 2:
            e.wait_ge(self.s[1], 16 * self.per * (n // 2))


class BatchDma:

    def __init__(self, pg, name):
        self.s = pg.sem(name)

    def inc(self, ins):
        ins.then_inc(self.s, 16)

    def wait_all(self, e, n):
        e.wait_ge(self.s, 16 * n)


class Prog:
    def __init__(self, ext_in, ext_out):
        self.nc = bass.Bass(target_bir_lowering=False)
        self.ext_in = set(ext_in)
        self.ext_out = set(ext_out)
        self.aps = {}
        self._ctx = []
        self.PS = [self.nc.alloc_psum_tensor(f"psb{i}", [128, 512], F32)
                   for i in range(8)]
        self.arena_base = ((SBUF_TOP - self.nc.sbuf_bytes_remaining) + 63) & ~63
        self.arena_cur = self.arena_base
        self.arena_hw = self.arena_base
        self.phases = []
        self.done = self.sem("DONE")

    def dram(self, name):
        if name not in self.aps:
            shape, dt = DRAM_REG[name]
            kind = ("ExternalInput" if name in self.ext_in else
                    "ExternalOutput" if name in self.ext_out else "Internal")
            self.aps[name] = self.nc.dram_tensor(name, shape, dt, kind=kind).ap()
        return self.aps[name]

    def sem(self, name):
        h = self.nc.semaphore(name)
        s = h.__enter__()
        self._ctx.append(h)
        return s

    def sb(self, name, shape, dtype):
        esz = mybir.dt.size(dtype)
        nbytes = int(np.prod(shape[1:])) * esz
        off = (self.arena_cur + 63) & ~63
        assert off + nbytes <= SBUF_TOP - 512, (
            f"SBUF arena overflow: {name} at {off}+{nbytes}")
        self.arena_cur = off + nbytes
        self.arena_hw = max(self.arena_hw, self.arena_cur)
        return self.nc.alloc_sbuf_tensor_at(name, shape, dtype, offset=off)

    def arena_reset(self):
        self.arena_cur = self.arena_base

    def ping(self, name, per=1):
        return PingDma(self, name, per)

    def batch(self, name):
        return BatchDma(self, name)

    def add_phase(self, eng_dict):
        self.phases.append(eng_dict)

    def finalize(self):
        nc = self.nc
        done = self.done
        phases = self.phases

        def run_engine(e, key):
            for i, ph in enumerate(phases):
                e.wait_ge(done, 5 * i)
                fn = ph.get(key)
                if fn:
                    fn(e)
                e.nop().then_inc(done, 1)

        blk = nc.Block()
        blk.__enter__()

        @blk.sync
        def _(e):
            run_engine(e, "sync")

        @blk.gpsimd
        def _(e):
            run_engine(e, "gpsimd")

        @blk.tensor
        def _(e):
            run_engine(e, "tensor")

        @blk.vector
        def _(e):
            run_engine(e, "vector")

        @blk.scalar
        def _(e):
            run_engine(e, "scalar")

        blk.__exit__(None, None, None)
        for h in reversed(self._ctx):
            h.__exit__(None, None, None)
        return nc



def ph_ln0(pg: Prog):
    x = pg.dram("x"); xh = pg.dram("xh")
    tln0 = pg.dram("tln0"); tlnh = pg.dram("tlnh")
    pd_x = pg.ping("l0A"); pd_o = pg.ping("l0B")
    sV = pg.sem("l0V"); sS = pg.sem("l0S")
    s_x = [pg.sb(f"l0x{i}", [128, D], F32) for i in range(2)]
    s_o = [pg.sb(f"l0o{i}", [128, D], BF16) for i in range(2)]
    s_cent = [pg.sb(f"l0c{i}", [128, D], F32) for i in range(2)]
    s_sq = pg.sb("l0sq", [128, D], F32)
    st = pg.sb("l0st", [128, 8], F32)
    NTILE = NT + 1

    def src(j):
        return (x[j * 128:(j + 1) * 128, :], 128) if j < NT else (xh[0:3, :], 3)

    def dst(j):
        return tln0[j * 128:(j + 1) * 128, :] if j < NT else tlnh[0:3, :]

    def esync(e):
        for j in range(NTILE + 1):
            if j < NTILE:
                if j >= 2:
                    e.wait_ge(sV, 3 * (j - 2) + 1)
                s, r = src(j)
                pd_x.inc(e.dma_start(out=s_x[j % 2][0:r, :], in_=s), j)
            if j >= 1:
                jj = j - 1
                e.wait_ge(sV, 3 * jj + 3)
                _, r = src(jj)
                pd_o.inc(e.dma_start(out=dst(jj), in_=s_o[jj % 2][0:r, :]), jj)
        pd_o.wait_all(e, NTILE)

    def evec(e):
        for j in range(NTILE):
            _, r = src(j)
            xb = s_x[j % 2][0:r, :]
            cb_ = s_cent[j % 2][0:r, :]
            pd_x.wait(e, j)
            e.reduce_sum(st[0:r, 0:1], xb, axis=AX.X)
            e.drain()
            e.tensor_scalar(st[0:r, 1:2], st[0:r, 0:1], 1.0 / D, None, ALU.mult)
            e.drain()
            e.tensor_scalar(cb_, xb, st[0:r, 1:2], None, ALU.subtract).then_inc(sV, 1)
            e.wait_ge(sS, 2 * j + 1)
            e.tensor_scalar(st[0:r, 3:4], st[0:r, 2:3], 1.0 / D, EPS, ALU.mult, ALU.add)
            e.drain()
            e.reciprocal(st[0:r, 4:5], st[0:r, 3:4]).then_inc(sV, 1)
            e.wait_ge(sS, 2 * j + 2)
            if j >= 2:
                pd_o.wait(e, j - 2)
            e.tensor_scalar(s_o[j % 2][0:r, :], cb_, st[0:r, 6:7], None,
                            ALU.mult).then_inc(sV, 1)
        e.wait_ge(sV, 3 * NTILE)

    def esca(e):
        for j in range(NTILE):
            _, r = src(j)
            e.wait_ge(sV, 3 * j + 1)
            e.activation(s_sq[0:r, :], s_cent[j % 2][0:r, :], AF.Square,
                         accum_out=st[0:r, 2:3]).then_inc(sS, 1)
            e.wait_ge(sV, 3 * j + 2)
            e.activation(st[0:r, 6:7], st[0:r, 4:5], AF.Sqrt).then_inc(sS, 1)
        e.wait_ge(sS, 2 * NTILE)

    return {"sync": esync, "vector": evec, "scalar": esca}



def ph_inproj(pg: Prog):
    tln0 = pg.dram("tln0"); tlnh = pg.dram("tlnh")
    wprojT = pg.dram("wprojT")
    cw = pg.dram("cw"); cb = pg.dram("cb"); dtb = pg.dram("dtb")
    gatefm = pg.dram("gatefm"); hsfm = pg.dram("hsfm")
    bcfm = pg.dram("bcfm"); dtac = pg.dram("dtac")

    pw = pg.ping("p1W"); bA = pg.batch("p1A"); bC = pg.batch("p1C")
    pg_st = pg.ping("p1GS"); ph_st = pg.ping("p1HS"); bD = pg.batch("p1DS")
    sT = pg.sem("p1T"); sFree = pg.sem("p1F"); sP = pg.sem("p1P")
    sGs = pg.sem("p1G"); sHs = pg.sem("p1H"); sDt = pg.sem("p1D")

    s_xt = pg.sb("p1xt", [128, 32 * 1024], BF16)
    s_xht = pg.sb("p1xht", [128, 32 * 3], BF16)
    s_wblk = [pg.sb(f"p1w{i}", [128, 32 * 512], BF16) for i in range(2)]
    s_g = [pg.sb(f"p1g{i}", [128, 1024], BF16) for i in range(2)]
    s_raw = [pg.sb(f"p1r{i}", [128, 1027], F32) for i in range(2)]
    s_cva = [pg.sb(f"p1cva{i}", [128, 1024], F32) for i in range(2)]
    s_cvb = [pg.sb(f"p1cvb{i}", [128, 1024], F32) for i in range(2)]
    s_hc = [pg.sb(f"p1hc{i}", [128, 1024], BF16) for i in range(2)]
    s_cw = pg.sb("p1cw", [128, 80 * 4], F32)
    s_cb = pg.sb("p1cb", [128, 80], F32)
    s_dtb = pg.sb("p1dtb", [128, 1], F32)
    s_dt1 = pg.sb("p1dt1", [128, 1024], F32)
    s_dt2 = pg.sb("p1dt2", [128, 1024], F32)
    s_dt = pg.sb("p1dt", [128, 1024], F32)

    NOG = 37
    NU = 145

    def og_units(g):
        return range(4 * g, min(4 * g + 4, NU))

    def esync(e):
        for d in range(32):
            bA.inc(e.dma_start_transpose(s_xt[:, d * 1024:(d + 1) * 1024],
                                         tln0[0:1024, d * 128:(d + 1) * 128]))
            with pg.nc.allow_non_contiguous_dma(reason="3-token halo transpose"):
                bA.inc(e.dma_start_transpose(s_xht[:, d * 3:(d + 1) * 3],
                                             tlnh[0:3, d * 128:(d + 1) * 128]))
        for u in range(NU):
            if u < 64:
                e.wait_ge(sGs, u + 1)
                pg_st.inc(e.dma_start(out=gatefm[u * 128:(u + 1) * 128, :],
                                      in_=s_g[u % 2][:, :]), u)
            elif u < 144:
                cc = u - 64
                e.wait_ge(sHs, cc + 1)
                if cc < 64:
                    o = hsfm[cc * 128:(cc + 1) * 128, :]
                else:
                    o = bcfm[(cc - 64) * 128:(cc - 63) * 128, :]
                ph_st.inc(e.dma_start(out=o, in_=s_hc[cc % 2][:, :]), cc)
            else:
                e.wait_ge(sDt, 2)
                bD.inc(e.dma_start(out=dtac[:, :], in_=s_dt[:, :]))
        pg_st.wait_all(e, 64)
        ph_st.wait_all(e, 80)
        bD.wait_all(e, 1)

    def conv_vec(e, cc):
        if cc == 0:
            bC.wait_all(e, 3)
        if cc >= 1:
            e.wait_ge(sHs, cc - 1)
        raw = s_raw[cc % 2]
        cva, cvb = s_cva[cc % 2], s_cvb[cc % 2]
        e.tensor_scalar(cva[:, :], raw[:, 0:1024], s_cw[:, cc * 4:cc * 4 + 1],
                        None, ALU.mult)
        e.drain()
        e.scalar_tensor_tensor(cvb[:, :], raw[:, 1:1025],
                               s_cw[:, cc * 4 + 1:cc * 4 + 2],
                               cva[:, :], ALU.mult, ALU.add)
        e.drain()
        e.scalar_tensor_tensor(cva[:, :], raw[:, 2:1026],
                               s_cw[:, cc * 4 + 2:cc * 4 + 3],
                               cvb[:, :], ALU.mult, ALU.add)
        e.drain()
        e.scalar_tensor_tensor(cvb[:, :], raw[:, 3:1027],
                               s_cw[:, cc * 4 + 3:cc * 4 + 4],
                               cva[:, :], ALU.mult, ALU.add).then_inc(sP, 1)

    def egps(e):
        bC.inc(e.dma_start(out=s_cw[:, :], in_=cw))
        bC.inc(e.dma_start(out=s_cb[:, :], in_=cb))
        bC.inc(e.dma_start(out=s_dtb[:, :], in_=dtb[:, :]))
        for g in range(NOG):
            if g >= 2:
                e.wait_ge(sT, max(0, 4 * g - 4))
            ow = 512 if g < 36 else 128
            o0 = g * 512
            buf = s_wblk[g % 2]
            pw.inc(e.dma_start(
                out=buf[:, 0:32 * ow].rearrange("p (dc o) -> p dc o", o=ow),
                in_=wprojT[:, o0:o0 + ow].rearrange("(dc p) o -> p dc o", p=128),
            ), g)
        pw.wait_all(e, NOG)
        bC.wait_all(e, 3)

    def etens(e):
        bA.wait_all(e, 64)
        for g in range(NOG):
            pw.wait(e, g)
            ow = 512 if g < 36 else 128
            buf = s_wblk[g % 2]
            for u in og_units(g):
                oc = u - 4 * g
                p0 = pg.PS[(u % 2) * 2]
                p1 = pg.PS[(u % 2) * 2 + 1]
                ph_ = pg.PS[4 + u % 2]
                if u >= 2:
                    if u - 2 < 64:
                        e.wait_ge(sGs, u - 1)
                    else:
                        e.wait_ge(sGs, 64)
                        e.wait_ge(sFree, u - 1 - 64)
                for d in range(32):
                    wt = buf[:, d * ow + oc * 128: d * ow + (oc + 1) * 128]
                    e.ldweights(wt)
                    e.matmul(p0[:, :], wt, s_xt[:, d * 1024:d * 1024 + 512],
                             start=(d == 0), stop=(d == 31))
                    mm = e.matmul(p1[:, :], wt,
                                  s_xt[:, d * 1024 + 512:(d + 1) * 1024],
                                  start=(d == 0), stop=(d == 31))
                    if 64 <= u < 144:
                        mm = e.matmul(ph_[:, 0:3], wt, s_xht[:, d * 3:(d + 1) * 3],
                                      start=(d == 0), stop=(d == 31))
                mm.then_inc(sT, 1)
        e.wait_ge(sT, NU)

    def evec(e):
        for u in range(64, 144):
            cc = u - 64
            e.wait_ge(sT, u + 1)
            p0 = pg.PS[(u % 2) * 2]
            p1 = pg.PS[(u % 2) * 2 + 1]
            ph_ = pg.PS[4 + u % 2]
            raw = s_raw[cc % 2]
            e.tensor_copy(raw[:, 0:3], ph_[:, 0:3])
            e.tensor_copy(raw[:, 3:515], p0[:, :])
            e.tensor_copy(raw[:, 515:1027], p1[:, :]).then_inc(sFree, 1)
            e.drain()
            conv_vec(e, cc)
        e.wait_ge(sDt, 1)
        e.tensor_add(s_dt[:, :], s_dt1[:, :], s_dt2[:, :]).then_inc(sDt, 1)
        e.wait_ge(sDt, 2)

    def esca(e):
        for u in range(NU):
            p0 = pg.PS[(u % 2) * 2]
            p1 = pg.PS[(u % 2) * 2 + 1]
            if u < 64:
                e.wait_ge(sT, u + 1)
                if u >= 2:
                    pg_st.wait(e, u - 2)
                e.activation(s_g[u % 2][:, 0:512], p0[:, :], AF.Silu)
                e.activation(s_g[u % 2][:, 512:1024], p1[:, :],
                             AF.Silu).then_inc(sGs, 1)
            elif u < 144:
                cc = u - 64
                if cc == 0:
                    bC.wait_all(e, 3)
                e.wait_ge(sP, cc + 1)
                if cc >= 2:
                    ph_st.wait(e, cc - 2)
                e.activation(s_hc[cc % 2][:, :], s_cvb[cc % 2][:, :], AF.Silu,
                             bias=s_cb[:, cc:cc + 1]).then_inc(sHs, 1)
            else:
                bC.wait_all(e, 3)
                e.wait_ge(sT, u + 1)
                for half, pp in ((0, p0), (1, p1)):
                    sl = slice(half * 512, half * 512 + 512)
                    e.activation(s_dt1[:, sl], pp[:, :], AF.Abs, bias=s_dtb[:, 0:1])
                    e.drain()
                    e.activation(s_dt1[:, sl], s_dt1[:, sl], AF.Exp, scale=-1.0)
                    e.drain()
                    e.activation(s_dt2[:, sl], s_dt1[:, sl], AF.Ln, bias=1.0)
                    e.drain()
                    last = e.activation(s_dt1[:, sl], pp[:, :], AF.Relu,
                                        bias=s_dtb[:, 0:1])
                last.then_inc(sDt, 1)
        e.wait_ge(sGs, 64)
        e.wait_ge(sDt, 1)

    return {"sync": esync, "gpsimd": egps, "tensor": etens,
            "vector": evec, "scalar": esca}



def ph_ssd1(pg: Prog):
    hsfm = pg.dram("hsfm"); bcfm = pg.dram("bcfm"); dtac = pg.dram("dtac")
    alog = pg.dram("alog"); coefc = pg.dram("coefc"); idtf = pg.dram("idtf")
    saux = pg.dram("saux"); acsr = pg.dram("acsr"); sleft = pg.dram("sleft")
    contrib = pg.dram("contrib"); gath = pg.dram("gath")

    bC4 = pg.batch("s1C4"); bBT = pg.batch("s1BT"); bACS = pg.batch("s1AC")
    pH = pg.ping("s1H"); bSX = pg.batch("s1SX"); bCT = pg.batch("s1CT")
    bGC = pg.batch("s1GC"); pSR = pg.ping("s1SR"); bSL = pg.batch("s1SL")
    sT = pg.sem("s1T"); sT1 = pg.sem("s1T1")
    sV = pg.sem("s1V"); sS = pg.sem("s1S"); sX = pg.sem("s1X")
    sU = pg.sem("s1U"); sCC = pg.sem("s1C")

    s_dt = pg.sb("s1dt", [128, T], F32)
    s_adt = pg.sb("s1adt", [128, T], F32)
    s_acs = pg.sb("s1acs", [128, T], F32)
    s_acst = pg.sb("s1acst", [128, T], F32)
    s_dtt = pg.sb("s1dtt", [128, T], F32)
    s_ea = pg.sb("s1ea", [128, T], F32)
    s_f = pg.sb("s1f", [128, T], F32)
    s_gmb = pg.sb("s1gmb", [128, T], F32)
    s_csra = pg.sb("s1csra", [1, T], F32)
    s_ftmp = pg.sb("s1ftmp", [128, 128], F32)
    s_ftmp2 = pg.sb("s1ftmp2", [128, 128], F32)
    s_al = pg.sb("s1al", [128, 1], F32)
    s_a = pg.sb("s1a", [128, 1], F32)
    s_idtf = pg.sb("s1idtf", [128, 128], F32)
    s_onesf = pg.sb("s1onesf", [1, 128], F32)
    s_coefc = pg.sb("s1coefc", [128, 40], F32)
    s_btall = pg.sb("s1bt", [128, 64 * 128], BF16)
    s_hst = [pg.sb(f"s1h{i}", [128, 4096], BF16) for i in range(2)]
    s_x2 = [pg.sb(f"s1x2{i}", [128, 64], BF16) for i in range(2)]
    s_prev = pg.sb("s1prev", [128, INTER], F32)
    s_srem = [pg.sb(f"s1sr{i}", [128, INTER], F32) for i in range(2)]
    s_gsum = pg.sb("s1gs", [128, 1], F32)
    s_growall = pg.sb("s1gw", [128, 4], F32)
    s_cw1 = pg.sb("s1cw1", [128, 1], F32)
    s_cw2 = pg.sb("s1cw2", [128, 1], F32)
    s_crow = pg.sb("s1cr", [1, 128], F32)
    s_cb = pg.sb("s1cb", [128, 128], F32)


    def esync(e):
        bC4.inc(e.dma_start(out=s_dt[:, :], in_=dtac))
        bC4.inc(e.dma_start(out=s_al[:, :], in_=alog))
        bC4.inc(e.dma_start(out=s_coefc[:, :], in_=coefc))
        bC4.inc(e.dma_start(out=s_idtf[:, :], in_=idtf))
        for c in range(NCH):
            for g in range(8):
                bBT.inc(e.dma_start_transpose(
                    s_btall[:, (c * 8 + g) * 128:(c * 8 + g + 1) * 128],
                    bcfm[g * 128:(g + 1) * 128, c * 128:(c + 1) * 128]))
        for c in range(NCH):
            e.wait_ge(sV, 2 + c)
            bACS.inc(e.dma_start(
                out=acsr[c:c + 1, :].rearrange("a (h t) -> (a h) t", h=128),
                in_=s_acs[:, c * 128:(c + 1) * 128]))
        for q in range(16):
            if q >= 2:
                e.wait_ge(sX, (q - 1) * 64)
            c, half = divmod(q, 2)
            pH.inc(e.dma_start_transpose(
                s_hst[q % 2][:, :],
                hsfm[half * 4096:(half + 1) * 4096, c * 128:(c + 1) * 128]), q)
        e.wait_ge(sV, 33)
        bSX.inc(e.dma_start(out=saux[0:128, :], in_=s_acst[:, :]))
        bSX.inc(e.dma_start(out=saux[128:256, :], in_=s_dtt[:, :]))
        e.wait_ge(sS, 2)
        bSX.inc(e.dma_start(out=saux[256:384, :], in_=s_ea[:, :]))
        e.wait_ge(sV, 49)
        bSX.inc(e.dma_start(out=saux[384:512, :], in_=s_f[:, :]))
        e.wait_ge(sS, 18)
        bSX.inc(e.dma_start(out=saux[512:640, :], in_=s_gmb[:, :]))
        e.wait_ge(sU, 1024)
        bCT.inc(e.dma_start(out=contrib[0:1, 0:STATESZ].rearrange(
            "a (n f) -> (a n) f", n=128), in_=s_prev[:, :]))
        e.wait_ge(sS, 19)
        bCT.inc(e.dma_start(out=contrib[0:1, STATESZ:STATESZ + 128].rearrange(
            "a (p o) -> (a p) o", o=1), in_=s_gsum[:, :]))
        e.wait_ge(sCC, 1)
        for r in range(4):
            bGC.inc(e.dma_start(out=s_growall[:, r:r + 1], in_=gath[
                r:r + 1, STATESZ:STATESZ + 128].rearrange(
                "a (p o) -> (a p) o", o=1)))
        for r in range(4):
            if r >= 2:
                e.wait_ge(sV, 54 + 4 * (r - 2))
            pSR.inc(e.dma_start(out=s_srem[r % 2][:, :], in_=gath[
                r:r + 1, 0:STATESZ].rearrange("a (n f) -> (a n) f", n=128)), r)
        e.wait_ge(sV, 66)
        bSL.inc(e.dma_start(out=sleft, in_=s_prev[:, :]))
        bACS.wait_all(e, NCH)
        bSX.wait_all(e, 5)
        bSL.wait_all(e, 1)

    def egps(e):
        bCT.wait_all(e, 2)
        e.collective_compute(
            "AllGather", ALU.bypass,
            replica_groups=[[0, 1, 2, 3], [4, 5, 6, 7]],
            ins=[contrib.opt()], outs=[gath.opt()],
        ).then_inc(sCC)
        e.wait_ge(sCC, 1)

    def etens(e):
        bC4.wait_all(e, 4)
        for k in range(24):
            c = k % 8
            if k >= 2:
                e.wait_ge(sV, 9 + (k - 1))
            pb = pg.PS[5 + k % 2]
            if k < 8:
                e.wait_ge(sV, 1)
                e.transpose(pb[:, 0:128], s_dt[:, c * 128:(c + 1) * 128],
                            s_idtf[:, :]).then_inc(sT, 1)
            elif k < 16:
                e.wait_ge(sV, 2 + c)
                e.transpose(pb[:, 0:128], s_acs[:, c * 128:(c + 1) * 128],
                            s_idtf[:, :]).then_inc(sT, 1)
            else:
                e.transpose(pb[0:1, 0:128],
                            s_acs[:, c * 128 + 127:c * 128 + 128],
                            s_idtf[:, :]).then_inc(sT, 1)
        for c in range(NCH):
            e.wait_ge(sV, 26 + c)
            if c >= 2:
                e.wait_ge(sS, 3 + 2 * (c - 2))
                e.wait_ge(sV, 34 + 2 * (c - 2))
            e.matmul(pg.PS[3 + c % 2][:, 0:128], s_onesf[0:1, :],
                     s_csra[0:1, c * 128:(c + 1) * 128],
                     start=True, stop=True).then_inc(sT, 1)
        bBT.wait_all(e, 64)
        for u in range(1024):
            c, h = divmod(u, 128)
            g = h % 8
            e.wait_ge(sX, u + 1)
            if u >= 2:
                e.wait_ge(sU, u - 1)
            e.matmul(pg.PS[u % 2][:, 0:64],
                     s_btall[:, (c * 8 + g) * 128:(c * 8 + g + 1) * 128],
                     s_x2[u % 2][:, :], start=True, stop=True).then_inc(sT1, 1)
        for r in range(4):
            e.wait_ge(sV, 51 + 4 * r)
            e.transpose(pg.PS[5 + r % 2][0:1, 0:128], s_cw1[:, :],
                        s_idtf[:, :]).then_inc(sT, 1)
            e.wait_ge(sV, 52 + 4 * r)
            e.matmul(pg.PS[3 + r % 2][:, 0:128], s_onesf[0:1, :],
                     s_crow[0:1, :], start=True, stop=True).then_inc(sT, 1)
        e.wait_ge(sT, 40)
        e.wait_ge(sT1, 1024)

    def evec(e):
        e.wait_ge(sS, 1)
        e.tensor_scalar(s_a[:, :], s_a[:, :], -1.0, None, ALU.mult)
        e.memset(s_onesf[:, :], 1.0)
        e.memset(s_prev[:, :], 0.0)
        e.drain()
        e.tensor_scalar(s_adt[:, :], s_dt[:, :], s_a[:, 0:1], None,
                        ALU.mult).then_inc(sV, 1)
        e.drain()
        for c in range(NCH):
            e.tensor_tensor_scan(s_acs[:, c * 128:(c + 1) * 128],
                                 s_adt[:, c * 128:(c + 1) * 128],
                                 s_adt[:, c * 128:(c + 1) * 128],
                                 0.0, ALU.add, ALU.bypass).then_inc(sV, 1)
        for k in range(24):
            c = k % 8
            e.wait_ge(sT, k + 1)
            pb = pg.PS[5 + k % 2]
            if k < 8:
                e.tensor_copy(s_dtt[:, c * 128:(c + 1) * 128],
                              pb[:, 0:128]).then_inc(sV, 1)
            elif k < 16:
                e.tensor_copy(s_acst[:, c * 128:(c + 1) * 128],
                              pb[:, 0:128]).then_inc(sV, 1)
            else:
                e.tensor_copy(s_csra[0:1, c * 128:(c + 1) * 128],
                              pb[0:1, 0:128]).then_inc(sV, 1)
        for c in range(NCH):
            e.wait_ge(sT, 25 + c)
            e.wait_ge(sS, 3 + 2 * c)
            e.tensor_tensor(s_ftmp[:, :], pg.PS[3 + c % 2][:, 0:128],
                            s_acst[:, c * 128:(c + 1) * 128],
                            ALU.subtract).then_inc(sV, 1)
            e.wait_ge(sS, 4 + 2 * c)
            e.tensor_tensor(s_f[:, c * 128:(c + 1) * 128], s_ftmp2[:, :],
                            s_dtt[:, c * 128:(c + 1) * 128],
                            ALU.mult).then_inc(sV, 1)
        for u in range(1024):
            c, h = divmod(u, 128)
            q = c * 2 + h // 64
            pH.wait(e, q)
            if u >= 2:
                e.wait_ge(sT1, u - 1)
            hh = h % 64
            e.tensor_scalar(s_x2[u % 2][:, :],
                            s_hst[q % 2][:, hh * 64:(hh + 1) * 64],
                            s_f[:, c * 128 + h:c * 128 + h + 1], None,
                            ALU.mult).then_inc(sX, 1)
            e.wait_ge(sT1, u + 1)
            e.scalar_tensor_tensor(s_prev[:, h * 64:(h + 1) * 64],
                                   s_prev[:, h * 64:(h + 1) * 64],
                                   s_gmb[:, c * 128 + h:c * 128 + h + 1],
                                   pg.PS[u % 2][:, 0:64],
                                   ALU.mult, ALU.add).then_inc(sU, 1)
        e.tensor_copy(s_gsum[:, :], s_acs[:, 127:128])
        for c in range(1, NCH):
            e.drain()
            e.tensor_add(s_gsum[:, :], s_gsum[:, :],
                         s_acs[:, c * 128 + 127:c * 128 + 128])
        e.drain()
        e.tensor_scalar(s_gsum[:, :], s_gsum[:, :], 1.0, None,
                        ALU.mult).then_inc(sV, 1)
        bGC.wait_all(e, 4)
        bCT.wait_all(e, 2)
        e.memset(s_prev[:, :], 0.0)
        e.drain()
        for r in range(4):
            e.tensor_scalar(s_cw1[:, :], s_growall[:, 0:1],
                            s_coefc[:, 4 * r:4 * r + 1],
                            s_coefc[:, 24 + 4 * r:25 + 4 * r],
                            ALU.mult, ALU.add)
            for m in range(1, 4):
                e.drain()
                e.tensor_scalar(s_cw2[:, :], s_growall[:, m:m + 1],
                                s_coefc[:, 4 * r + m:4 * r + m + 1],
                                s_coefc[:, 24 + 4 * r + m:25 + 4 * r + m],
                                ALU.mult, ALU.add)
                e.drain()
                e.tensor_tensor(s_cw1[:, :], s_cw1[:, :], s_cw2[:, :],
                                ALU.mult)
            e.drain()
            e.tensor_scalar(s_cw1[:, :], s_cw1[:, :],
                            s_coefc[:, 16 + r:17 + r], None,
                            ALU.mult).then_inc(sV, 1)
            e.wait_ge(sT, 33 + 2 * r)
            e.tensor_copy(s_crow[0:1, :],
                          pg.PS[5 + r % 2][0:1, 0:128]).then_inc(sV, 1)
            e.wait_ge(sT, 34 + 2 * r)
            e.tensor_copy(s_cb[:, :], pg.PS[3 + r % 2][:, 0:128]).then_inc(sV, 1)
            e.drain()
            pSR.wait(e, r)
            for h in range(128):
                ins = e.scalar_tensor_tensor(
                    s_prev[:, h * 64:(h + 1) * 64],
                    s_srem[r % 2][:, h * 64:(h + 1) * 64],
                    s_cb[:, h:h + 1],
                    s_prev[:, h * 64:(h + 1) * 64],
                    ALU.mult, ALU.add)
            ins.then_inc(sV, 1)
        e.wait_ge(sV, 66)

    def esca(e):
        bC4.wait_all(e, 4)
        e.activation(s_a[:, :], s_al[:, :], AF.Exp).then_inc(sS, 1)
        e.wait_ge(sV, 25)
        e.activation(s_ea[:, :], s_acst[:, :], AF.Exp).then_inc(sS, 1)
        for c in range(NCH):
            e.wait_ge(sT, 25 + c)
            e.activation(s_gmb[:, c * 128:(c + 1) * 128],
                         pg.PS[3 + c % 2][:, 0:128], AF.Exp).then_inc(sS, 1)
            e.wait_ge(sV, 34 + 2 * c)
            e.activation(s_ftmp2[:, :], s_ftmp[:, :], AF.Exp).then_inc(sS, 1)
        e.wait_ge(sV, 50)
        e.activation(s_gsum[:, :], s_gsum[:, :], AF.Exp).then_inc(sS, 1)
        e.wait_ge(sS, 19)

    return {"sync": esync, "gpsimd": egps, "tensor": etens,
            "vector": evec, "scalar": esca}



def ph_ssd2(pg: Prog):
    import os
    LVL = int(os.environ.get("SSD2_LVL", "3"))
    hsfm = pg.dram("hsfm"); bcfm = pg.dram("bcfm"); gatefm = pg.dram("gatefm")
    saux = pg.dram("saux"); acsr = pg.dram("acsr"); sleft = pg.dram("sleft")
    maskneg = pg.dram("maskneg"); ddr = pg.dram("ddr")
    mask01v = pg.dram("mask01")
    norm = pg.dram("norm")

    bP = pg.batch("s2P"); pBC = pg.ping("s2BC"); pBT = pg.ping("s2BT", per=8)
    pAR = pg.ping("s2AR"); pHS = pg.ping("s2HS"); pGT = pg.ping("s2GT")
    pST = pg.ping("s2ST")
    sTG = pg.sem("s2TG"); sTB = pg.sem("s2TB"); sTY = pg.sem("s2TY")
    sTS = pg.sem("s2TS"); sD2 = pg.sem("s2D2"); sE = pg.sem("s2E")
    sS2 = pg.sem("s2S2"); sX2 = pg.sem("s2X2"); sYO = pg.sem("s2YO")
    sPS = pg.sem("s2PS"); sPB = pg.sem("s2PB"); sXG = pg.sem("s2XG")
    sSQ = pg.sem("s2SQ"); sMS = pg.sem("s2MS"); sRS = pg.sem("s2RS")
    sNM = pg.sem("s2NM"); sDB = pg.sem("s2DB"); sGM = pg.sem("s2GM")

    s_acst = pg.sb("s2acst", [128, T], F32)
    s_dtt = pg.sb("s2dtt", [128, T], F32)
    s_ea = pg.sb("s2ea", [128, T], F32)
    s_f = pg.sb("s2f", [128, T], F32)
    s_gmb = pg.sb("s2gmb", [128, T], F32)
    s_mn = pg.sb("s2mn", [128, 128], F32)
    s_m01 = pg.sb("s2m01", [128, 128], F32)
    s_onesf = pg.sb("s2onesf", [1, 128], F32)
    s_ddr = pg.sb("s2ddr", [1, 128], F32)
    s_ddb = pg.sb("s2ddb", [128, 128], F32)
    s_arow = [pg.sb(f"s2ar{i}", [1, 2048], F32) for i in range(2)]
    s_bc = [pg.sb(f"s2bc{i}", [128, 2048], BF16) for i in range(2)]
    s_bt = [pg.sb(f"s2bt{i}", [128, 1024], BF16) for i in range(2)]
    s_hst = [pg.sb(f"s2h{i}", [128, 4096], BF16) for i in range(2)]
    s_gt = [pg.sb(f"s2g{i}", [128, 4096], BF16) for i in range(2)]
    s_prev = pg.sb("s2prev", [128, INTER], F32)
    s_prevb = [pg.sb(f"s2pb{i}", [128, INTER], BF16) for i in range(2)]
    s_y = [pg.sb(f"s2y{i}", [128, INTER], BF16) for i in range(2)]
    s_xg = pg.sb("s2xg", [128, INTER], BF16)
    s_d2 = [pg.sb(f"s2d2{i}", [128, 128], F32) for i in range(2)]
    s_e = [pg.sb(f"s2e{i}", [128, 128], BF16) for i in range(2)]
    s_s2 = [pg.sb(f"s2s2{i}", [128, 128], BF16) for i in range(2)]
    s_gm = [pg.sb(f"s2gm{i}", [128, 1024], BF16) for i in range(2)]
    s_x2 = [pg.sb(f"s2x2{i}", [128, 64], BF16) for i in range(2)]
    s_xd = [pg.sb(f"s2xd{i}", [128, 64], BF16) for i in range(2)]
    s_ss = pg.sb("s2ss", [128, 8], F32)
    s_rstd = pg.sb("s2rstd", [128, 8], F32)
    s_sqd = pg.sb("s2sqd", [128, 1024], F32)

    def esync(e):
        bP.inc(e.dma_start(out=s_acst[:, :], in_=saux[0:128, :]))
        bP.inc(e.dma_start(out=s_dtt[:, :], in_=saux[128:256, :]))
        bP.inc(e.dma_start(out=s_ea[:, :], in_=saux[256:384, :]))
        bP.inc(e.dma_start(out=s_f[:, :], in_=saux[384:512, :]))
        bP.inc(e.dma_start(out=s_gmb[:, :], in_=saux[512:640, :]))
        bP.inc(e.dma_start(out=s_prev[:, :], in_=sleft))
        bP.inc(e.dma_start(out=s_mn[:, :], in_=maskneg))
        bP.inc(e.dma_start(out=s_m01[:, :], in_=mask01v))
        bP.inc(e.dma_start(out=s_ddr[:, :], in_=ddr))
        for c in range(NCH):
            for half in range(2):
                q = c * 2 + half
                if q >= 2:
                    e.wait_ge(sPS, (q - 1) * 64)
                pHS.inc(e.dma_start_transpose(
                    s_hst[q % 2][:, :],
                    hsfm[half * 4096:(half + 1) * 4096,
                         c * 128:(c + 1) * 128]), q)
            if c >= 2:
                e.wait_ge(sTY, (c - 1) * 128)
            pBC.inc(e.dma_start(
                out=s_bc[c % 2][:, :].rearrange("p (r t) -> p r t", t=128),
                in_=bcfm[:, c * 128:(c + 1) * 128].rearrange(
                    "(r p) t -> p r t", p=128)), c)
            if c >= 2:
                e.wait_ge(sTS, (c - 1) * 128)
            for g in range(8):
                pBT.inc(e.dma_start_transpose(
                    s_bt[c % 2][:, g * 128:(g + 1) * 128],
                    bcfm[g * 128:(g + 1) * 128, c * 128:(c + 1) * 128]), c)
            for g in range(8):
                gidx = c * 8 + g
                if gidx >= 2:
                    e.wait_ge(sTB, (gidx - 1) * 16)
                pAR.inc(e.dma_start(
                    out=s_arow[g % 2][0:1, :],
                    in_=acsr[c:c + 1, g * 2048:(g + 1) * 2048]), gidx)
            if LVL >= 2:
                for half in range(2):
                    qg = c * 2 + half
                    if qg >= 2:
                        e.wait_ge(sXG, qg - 1)
                    pGT.inc(e.dma_start_transpose(
                        s_gt[qg % 2][:, :],
                        gatefm[half * 4096:(half + 1) * 4096,
                               c * 128:(c + 1) * 128]), qg)
            if c >= 1:
                e.wait_ge(sNM, c)
                pST.inc(e.dma_start(out=norm[(c - 1) * 128:c * 128, :],
                                    in_=s_y[(c - 1) % 2][:, :]), c - 1)
        e.wait_ge(sNM, NCH)
        pST.inc(e.dma_start(out=norm[(NCH - 1) * 128:NCH * 128, :],
                            in_=s_y[(NCH - 1) % 2][:, :]), NCH - 1)
        pST.wait_all(e, NCH)

    def etens(e):
        bP.wait_all(e, 9)
        e.wait_ge(sPB, 128)
        e.matmul(pg.PS[1][:, 0:128], s_onesf[0:1, :], s_ddr[0:1, :],
                 start=True, stop=True).then_inc(sDB, 1)
        e.wait_ge(sDB, 2)
        for u in range(1024):
            c, h = divmod(u, 128)
            g, hh = divmod(h, 16)
            gbc = h % 8
            gidx = c * 8 + g
            q = c * 2 + h // 64
            if h == 0:
                pBC.wait(e, c)
                pBT.wait(e, c)
                for g8 in range(8):
                    gg = c * 8 + g8
                    if gg >= 1:
                        e.wait_ge(sGM, gg)
                    e.matmul(pg.PS[0][:, 0:128],
                             s_bc[c % 2][:, g8 * 128:(g8 + 1) * 128],
                             s_bc[c % 2][:, (8 + g8) * 128:(9 + g8) * 128],
                             start=True, stop=True).then_inc(sTG, 1)
            if hh == 0:
                pAR.wait(e, gidx)
            if u >= 2:
                e.wait_ge(sD2, u - 1)
            e.matmul(pg.PS[1 + u % 2][:, 0:128], s_onesf[0:1, :],
                     s_arow[g % 2][0:1, hh * 128:(hh + 1) * 128],
                     start=True, stop=True).then_inc(sTB, 1)
            e.wait_ge(sS2, u + 1)
            if h % 64 == 0:
                pHS.wait(e, q)
            if u >= 2:
                e.wait_ge(sYO, u - 1)
            hq = (h % 64) * 64
            e.matmul(pg.PS[3 + u % 2][:, 0:64], s_s2[u % 2][:, :],
                     s_hst[q % 2][:, hq:hq + 64], start=True, stop=True)
            if h == 0:
                e.wait_ge(sPB, (c + 1) * 128)
            e.matmul(pg.PS[5 + u % 2][:, 0:64],
                     s_bc[c % 2][:, (8 + gbc) * 128:(9 + gbc) * 128],
                     s_prevb[c % 2][:, h * 64:(h + 1) * 64],
                     start=True, stop=True).then_inc(sTY, 1)
            e.wait_ge(sX2, u + 1)
            if u >= 1:
                e.wait_ge(sPS, u)
            e.matmul(pg.PS[7][:, 0:64],
                     s_bt[c % 2][:, gbc * 128:(gbc + 1) * 128],
                     s_x2[u % 2][:, :], start=True, stop=True).then_inc(sTS, 1)
        e.wait_ge(sTS, 1024)

    def evec(e):
        bP.wait_all(e, 9)
        e.memset(s_onesf[:, :], 1.0)
        e.drain()
        e.tensor_copy(s_prevb[0][:, :], s_prev[:, :]).then_inc(sPB, 128)
        e.wait_ge(sDB, 1)
        e.tensor_copy(s_ddb[:, :], pg.PS[1][:, 0:128]).then_inc(sDB, 1)

        def yout2(u2):
            c2, h2 = divmod(u2, 128)
            e.scalar_tensor_tensor(s_y[c2 % 2][:, h2 * 64:(h2 + 1) * 64],
                                   pg.PS[3 + u2 % 2][:, 0:64], 1.0,
                                   s_y[c2 % 2][:, h2 * 64:(h2 + 1) * 64],
                                   ALU.mult, ALU.add).then_inc(sYO, 1)

        def norm_block(c2):
            if LVL < 3:
                e.engine_nop().then_inc(sNM, 1)
                return
            e.wait_ge(sSQ, c2 + 1)
            e.tensor_scalar(s_ss[:, :], s_ss[:, :], 1.0 / 1024.0, EPS,
                            ALU.mult, ALU.add).then_inc(sMS, 1)
            e.wait_ge(sRS, c2 + 1)
            for gi in range(8):
                ins = e.tensor_scalar(
                    s_y[c2 % 2][:, gi * 1024:(gi + 1) * 1024],
                    s_xg[:, gi * 1024:(gi + 1) * 1024],
                    s_rstd[:, gi:gi + 1], None, ALU.mult)
            ins.then_inc(sNM, 1)

        for u in range(1024):
            c, h = divmod(u, 128)
            g, hh = divmod(h, 16)
            gbc = h % 8
            gidx = c * 8 + g
            q = c * 2 + h // 64
            hq = (h % 64) * 64
            col = slice(c * 128 + h, c * 128 + h + 1)
            if h == 0:
                if c >= 2:
                    e.wait_ge(sS2, (c - 1) * 128)
                for g8 in range(8):
                    gg = c * 8 + g8
                    e.wait_ge(sTG, gg + 1)
                    e.tensor_tensor(s_gm[c % 2][:, g8 * 128:(g8 + 1) * 128],
                                    pg.PS[0][:, 0:128],
                                    s_m01[:, :], ALU.mult).then_inc(sGM, 1)
            e.wait_ge(sTB, u + 1)
            if u >= 2:
                e.wait_ge(sE, u - 1)
            e.tensor_scalar(s_d2[u % 2][:, :], pg.PS[1 + u % 2][:, 0:128],
                            s_acst[:, col], None, ALU.subtract)
            e.drain()
            e.tensor_add(s_d2[u % 2][:, :], s_d2[u % 2][:, :],
                         s_mn[:, :]).then_inc(sD2, 1)
            if u >= 1:
                yout2(u - 1)
                if h == 0:
                    norm_block(c - 1)
            if h % 64 == 0:
                pHS.wait(e, q)
            e.tensor_scalar(s_xd[u % 2][:, :], s_hst[q % 2][:, hq:hq + 64],
                            s_ddb[:, h:h + 1], None, ALU.mult)
            e.wait_ge(sE, u + 1)
            e.scalar_tensor_tensor(s_s2[u % 2][:, :], s_e[u % 2][:, :],
                                   s_dtt[:, col],
                                   s_gm[c % 2][:, gbc * 128:(gbc + 1) * 128],
                                   ALU.mult, ALU.mult).then_inc(sS2, 1)
            if u >= 2:
                e.wait_ge(sTS, u - 1)
            e.tensor_scalar(s_x2[u % 2][:, :], s_hst[q % 2][:, hq:hq + 64],
                            s_f[:, col], None, ALU.mult).then_inc(sX2, 1)
            e.wait_ge(sTY, u + 1)
            if c >= 2 and h == 0:
                pST.wait(e, c - 2)
            e.scalar_tensor_tensor(s_y[c % 2][:, h * 64:(h + 1) * 64],
                                   pg.PS[5 + u % 2][:, 0:64],
                                   s_ea[:, col], s_xd[u % 2][:, :],
                                   ALU.mult, ALU.add)
            e.wait_ge(sTS, u + 1)
            e.scalar_tensor_tensor(s_prev[:, h * 64:(h + 1) * 64],
                                   s_prev[:, h * 64:(h + 1) * 64],
                                   s_gmb[:, col], pg.PS[7][:, 0:64],
                                   ALU.mult, ALU.add).then_inc(sPS, 1)
        e.drain()
        yout2(1023)
        norm_block(NCH - 1)
        e.wait_ge(sNM, NCH)

    def egps(e):
        for u in range(1024):
            c, h = divmod(u, 128)
            e.wait_ge(sPS, u + 1)
            e.tensor_copy(s_prevb[(c + 1) % 2][:, h * 64:(h + 1) * 64],
                          s_prev[:, h * 64:(h + 1) * 64]).then_inc(sPB, 1)
            if h == 127 and LVL >= 2:
                e.wait_ge(sYO, (c + 1) * 128)
                if c >= 1:
                    e.wait_ge(sNM, c)
                for half in range(2):
                    pGT.wait(e, c * 2 + half)
                    e.tensor_tensor(
                        s_xg[:, half * 4096:(half + 1) * 4096],
                        s_y[c % 2][:, half * 4096:(half + 1) * 4096],
                        s_gt[(c * 2 + half) % 2][:, :],
                        ALU.mult).then_inc(sXG, 1)
        if LVL >= 2:
            e.wait_ge(sXG, 2 * NCH)

    def esca(e):
        for u in range(1024):
            c, h = divmod(u, 128)
            e.wait_ge(sD2, u + 1)
            if u >= 2:
                e.wait_ge(sS2, u - 1)
            e.activation(s_e[u % 2][:, :], s_d2[u % 2][:, :],
                         AF.Exp).then_inc(sE, 1)
            if h == 127 and LVL >= 3:
                e.wait_ge(sXG, 2 * (c + 1))
                if c >= 1:
                    e.wait_ge(sMS, c)
                for gi in range(8):
                    if gi:
                        e.drain()
                    ins = e.activation(s_sqd[:, :],
                                       s_xg[:, gi * 1024:(gi + 1) * 1024],
                                       AF.Square, accum_out=s_ss[:, gi:gi + 1])
                ins.then_inc(sSQ, 1)
                e.wait_ge(sMS, c + 1)
                e.activation(s_rstd[:, :], s_ss[:, :], AF.Ln)
                e.drain()
                e.activation(s_rstd[:, :], s_rstd[:, :], AF.Exp,
                             scale=-0.5).then_inc(sRS, 1)
        if LVL >= 3:
            e.wait_ge(sRS, NCH)

    return {"sync": esync, "gpsimd": egps, "tensor": etens,
            "vector": evec, "scalar": esca}



def ph_outproj(pg: Prog):
    normd = pg.dram("norm"); w2T = pg.dram("w2T"); nw = pg.dram("nw")
    abuf = pg.dram("abuf")
    bN = pg.batch("opN"); bC = pg.batch("opC"); pW = pg.ping("opW")
    pST = pg.ping("opST")
    sNW = pg.sem("opNW"); sT = pg.sem("opT"); sV = pg.sem("opV")
    sTH = pg.sem("opTH")

    s_nt = pg.sb("opnt", [128, 64 * 8 * 128], BF16)
    s_nw = pg.sb("opnw", [128, 64], F32)
    s_w = [pg.sb(f"opw{i}", [128, 32 * 512], BF16) for i in range(2)]
    s_o = [pg.sb(f"opo{i}", [128, 512], BF16) for i in range(2)]

    def nt(i, t):
        return s_nt[:, (i * 8 + t) * 128:(i * 8 + t + 1) * 128]

    def esync(e):
        bC.inc(e.dma_start(out=s_nw[:, :], in_=nw))
        for i in range(64):
            for t in range(8):
                bN.inc(e.dma_start_transpose(
                    nt(i, t), normd[t * 128:(t + 1) * 128,
                                    i * 128:(i + 1) * 128]))
        for u in range(64):
            e.wait_ge(sV, u + 1)
            o, t = divmod(u, 8)
            pST.inc(e.dma_start(
                out=abuf[t * 128:(t + 1) * 128, o * 512:(o + 1) * 512],
                in_=s_o[u % 2][:, :]), u)
        pST.wait_all(e, 64)

    def egps(e):
        for hb in range(16):
            o, ih = divmod(hb, 2)
            if hb >= 2:
                e.wait_ge(sTH, hb - 1)
            pW.inc(e.dma_start(
                out=s_w[hb % 2][:, :].rearrange("p (i f) -> p i f", f=512),
                in_=w2T[ih * 4096:(ih + 1) * 4096,
                        o * 512:(o + 1) * 512].rearrange(
                    "(i p) f -> p i f", p=128)), hb)
        pW.wait_all(e, 16)

    def evec(e):
        bN.wait_all(e, 512)
        bC.wait_all(e, 1)
        for i in range(64):
            for t in range(8):
                ins = e.tensor_scalar(nt(i, t), nt(i, t), s_nw[:, i:i + 1],
                                      None, ALU.mult)
        ins.then_inc(sNW, 1)
        for u in range(64):
            o, t = divmod(u, 8)
            e.wait_ge(sT, u + 1)
            if u >= 2:
                pST.wait(e, u - 2)
            e.tensor_copy(s_o[u % 2][:, :], pg.PS[t][:, :]).then_inc(sV, 1)
        e.wait_ge(sV, 64)

    def etens(e):
        e.wait_ge(sNW, 1)
        for hb in range(16):
            o, ih = divmod(hb, 2)
            pW.wait(e, hb)
            for ii in range(32):
                i = ih * 32 + ii
                for t in range(8):
                    if i == 0 and o >= 1:
                        e.wait_ge(sV, (o - 1) * 8 + t + 1)
                    e.ldweights(nt(i, t))
                    mm = e.matmul(pg.PS[t][:, :], nt(i, t),
                                  s_w[hb % 2][:, ii * 512:(ii + 1) * 512],
                                  start=(i == 0), stop=(i == 63))
                    if i == 63:
                        mm.then_inc(sT, 1)
            if ih == 0:
                mm.then_inc(sTH, 1)
            else:
                e.wait_ge(sT, o * 8 + 8)
                e.nop().then_inc(sTH, 1)
        e.wait_ge(sT, 64)
        e.wait_ge(sTH, 16)

    return {"sync": esync, "gpsimd": egps, "tensor": etens, "vector": evec}


def make_bound(tag, a_name, res_in_name, res_out_name, tln_out_name,
               final=False):

    def ph(pg: Prog):
        a_d = pg.dram(a_name)
        rin = pg.dram(res_in_name)
        rout = pg.dram(res_out_name)
        tout = pg.dram(tln_out_name) if not final else None
        pA = pg.ping(tag + "A"); pR = pg.ping(tag + "R")
        pO = pg.ping(tag + "O"); pT = pg.ping(tag + "T")
        sV = pg.sem(tag + "V"); sS = pg.sem(tag + "S")
        s_a = [pg.sb(f"{tag}a{i}", [128, D], BF16) for i in range(2)]
        s_r = [pg.sb(f"{tag}r{i}", [128, D], F32) for i in range(2)]
        s_x = [pg.sb(f"{tag}x{i}", [128, D], F32) for i in range(2)]
        s_t = [pg.sb(f"{tag}t{i}", [128, D], BF16) for i in range(2)]
        s_c = [pg.sb(f"{tag}c{i}", [128, D], F32) for i in range(2)]
        s_sq = pg.sb(f"{tag}sq", [128, D], F32)
        st = pg.sb(f"{tag}st", [128, 16], F32)
        NV = 4 if final else 6
        NS = 2 if final else 4

        def esync(e):
            for j in range(NT):
                if j >= 2:
                    e.wait_ge(sV, NV * (j - 2) + 1)
                pA.inc(e.dma_start(out=s_a[j % 2][:, :],
                                   in_=a_d[j * 128:(j + 1) * 128, :]), j)
                if j >= 2:
                    e.wait_ge(sV, NV * (j - 2) + 2)
                pR.inc(e.dma_start(out=s_r[j % 2][:, :],
                                   in_=rin[j * 128:(j + 1) * 128, :]), j)
                e.wait_ge(sV, NV * j + (3 if not final else NV))
                pO.inc(e.dma_start(out=rout[j * 128:(j + 1) * 128, :],
                                   in_=s_x[j % 2][:, :]), j)
                if not final:
                    e.wait_ge(sV, NV * j + 6)
                    pT.inc(e.dma_start(out=tout[j * 128:(j + 1) * 128, :],
                                       in_=s_t[j % 2][:, :]), j)
            pO.wait_all(e, NT)
            if not final:
                pT.wait_all(e, NT)

        def evec(e):
            for j in range(NT):
                a = s_a[j % 2]; r = s_r[j % 2]; xx = s_x[j % 2]
                cb_ = s_c[j % 2]
                pA.wait(e, j)
                e.reduce_sum(st[:, 0:1], a[:, :], axis=AX.X)
                e.drain()
                e.tensor_scalar(st[:, 1:2], st[:, 0:1], 1.0 / D, None, ALU.mult)
                e.drain()
                e.tensor_scalar(cb_[:, :], a[:, :], st[:, 1:2], None,
                                ALU.subtract).then_inc(sV, 1)
                e.wait_ge(sS, NS * j + 1)
                e.tensor_scalar(st[:, 3:4], st[:, 2:3], 1.0 / D, EPS,
                                ALU.mult, ALU.add)
                e.drain()
                e.reciprocal(st[:, 4:5], st[:, 3:4]).then_inc(sV, 1)
                e.wait_ge(sS, NS * j + 2)
                pR.wait(e, j)
                if j >= 2:
                    pO.wait(e, j - 2)
                e.scalar_tensor_tensor(xx[:, :], cb_[:, :], st[:, 6:7],
                                       r[:, :], ALU.mult,
                                       ALU.add).then_inc(sV, 1)
                if final:
                    e.engine_nop().then_inc(sV, 1)
                    continue
                e.drain()
                e.reduce_sum(st[:, 8:9], xx[:, :], axis=AX.X)
                e.drain()
                e.tensor_scalar(st[:, 9:10], st[:, 8:9], 1.0 / D, None,
                                ALU.mult)
                e.drain()
                e.tensor_scalar(cb_[:, :], xx[:, :], st[:, 9:10], None,
                                ALU.subtract).then_inc(sV, 1)
                e.wait_ge(sS, NS * j + 3)
                e.tensor_scalar(st[:, 11:12], st[:, 10:11], 1.0 / D, EPS,
                                ALU.mult, ALU.add)
                e.drain()
                e.reciprocal(st[:, 12:13], st[:, 11:12]).then_inc(sV, 1)
                e.wait_ge(sS, NS * j + 4)
                if j >= 2:
                    pT.wait(e, j - 2)
                e.tensor_scalar(s_t[j % 2][:, :], cb_[:, :], st[:, 14:15],
                                None, ALU.mult).then_inc(sV, 1)
            e.wait_ge(sV, NV * NT)

        def esca(e):
            for j in range(NT):
                cb_ = s_c[j % 2]
                e.wait_ge(sV, NV * j + 1)
                e.activation(s_sq[:, :], cb_[:, :], AF.Square,
                             accum_out=st[:, 2:3]).then_inc(sS, 1)
                e.wait_ge(sV, NV * j + 2)
                e.activation(st[:, 6:7], st[:, 4:5], AF.Sqrt).then_inc(sS, 1)
                if final:
                    continue
                e.wait_ge(sV, NV * j + 4)
                e.activation(s_sq[:, :], cb_[:, :], AF.Square,
                             accum_out=st[:, 10:11]).then_inc(sS, 1)
                e.wait_ge(sV, NV * j + 5)
                e.activation(st[:, 14:15], st[:, 12:13],
                             AF.Sqrt).then_inc(sS, 1)
            e.wait_ge(sS, NS * NT)

        return {"sync": esync, "vector": evec, "scalar": esca}

    return ph



def ph_ffn0(pg: Prog):
    tln1 = pg.dram("tln1"); fwT = pg.dram("fwT"); fbias = pg.dram("fbias")
    gwT = pg.dram("gwT"); ics = pg.dram("ics"); ocs = pg.dram("ocs")
    osd = pg.dram("osd"); fbuf = pg.dram("fbuf")
    idtf = pg.dram("idtf")

    bX = pg.batch("f0X"); bC = pg.batch("f0C"); pW = pg.ping("f0W")
    pOSs = pg.ping("f0OS"); pOSl = pg.ping("f0OL"); pFB = pg.ping("f0FB")
    sTR = pg.sem("f0TR"); sVR = pg.sem("f0VR"); sSR = pg.sem("f0SR")
    sTT = pg.sem("f0TT"); sTS = pg.sem("f0TS"); sXS = pg.sem("f0XS")
    sTO = pg.sem("f0TO"); sVO = pg.sem("f0VO"); sSG = pg.sem("f0SG")
    sTI = pg.sem("f0TI"); sTD = pg.sem("f0TD"); sV = pg.sem("f0V")

    s_xt = pg.sb("f0xt", [128, 32 * 1024], BF16)
    s_gw = pg.sb("f0gw", [128, 32 * 8], BF16)
    s_ics = pg.sb("f0ics", [8, 4096], BF16)
    s_ocs = pg.sb("f0ocs", [8, 4096], BF16)
    s_fb = pg.sb("f0fb", [1, 4096], BF16)
    s_oner = pg.sb("f0on", [1, 128], BF16)
    s_lg = pg.sb("f0lg", [128, 8], F32)
    s_el = pg.sb("f0el", [128, 8], F32)
    s_rout = pg.sb("f0rt", [128, 64], F32)
    s_rtt = pg.sb("f0rtt", [8, 1024], BF16)
    s_idtf = pg.sb("f0id", [128, 128], F32)
    st = pg.sb("f0st", [128, 8], F32)
    s_w = [pg.sb(f"f0w{i}", [128, 16 * 512], BF16) for i in range(2)]
    s_sg = [pg.sb(f"f0sg{i}", [128, 512], BF16) for i in range(2)]
    s_osb = [pg.sb(f"f0ob{i}", [128, 512], BF16) for i in range(2)]
    s_os = [pg.sb(f"f0os{i}", [128, 512], BF16) for i in range(2)]
    s_fo = [pg.sb(f"f0fo{i}", [128, 512], BF16) for i in range(2)]

    def xt(i, t):
        return s_xt[:, i * 1024 + t * 128:i * 1024 + (t + 1) * 128]

    def esync(e):
        for d in range(32):
            bX.inc(e.dma_start_transpose(
                s_xt[:, d * 1024:(d + 1) * 1024],
                tln1[0:1024, d * 128:(d + 1) * 128]))
        for u in range(64):
            o, t = divmod(u, 8)
            e.wait_ge(sVO, u + 1)
            pOSs.inc(e.dma_start(
                out=osd[t * 128:(t + 1) * 128, o * 512:(o + 1) * 512],
                in_=s_osb[u % 2][:, :]), u)
        for u in range(66):
            if u < 64:
                o, t = divmod(u, 8)
                pOSs.wait(e, u)
                if u >= 2:
                    e.wait_ge(sV, u - 1)
                pOSl.inc(e.dma_start(
                    out=s_os[u % 2][:, :],
                    in_=osd[t * 128:(t + 1) * 128, o * 512:(o + 1) * 512]), u)
            if u >= 2:
                us = u - 2
                o, t = divmod(us, 8)
                e.wait_ge(sV, us + 1)
                pFB.inc(e.dma_start(
                    out=fbuf[t * 128:(t + 1) * 128, o * 512:(o + 1) * 512],
                    in_=s_fo[us % 2][:, :]), us)
        pFB.wait_all(e, 64)

    def egps(e):
        bC.inc(e.dma_start(out=s_gw[:, :].rearrange("p (i o) -> p i o", o=8),
                           in_=gwT.rearrange("(i p) o -> p i o", p=128)))
        bC.inc(e.dma_start(out=s_idtf[:, :], in_=idtf))
        bC.inc(e.dma_start(out=s_ics[:, :], in_=ics))
        bC.inc(e.dma_start(out=s_ocs[:, :], in_=ocs))
        bC.inc(e.dma_start(out=s_fb[:, :], in_=fbias))
        for hb in range(16):
            o, ih = divmod(hb, 2)
            if hb >= 2:
                e.wait_ge(sTI, (hb - 1) * 16)
            pW.inc(e.dma_start(
                out=s_w[hb % 2][:, :].rearrange("p (i f) -> p i f", f=512),
                in_=fwT[ih * 2048:(ih + 1) * 2048,
                        o * 512:(o + 1) * 512].rearrange(
                    "(i p) f -> p i f", p=128)), hb)
        pW.wait_all(e, 16)

    def etens(e):
        bX.wait_all(e, 32)
        bC.wait_all(e, 5)
        for t in range(8):
            for i in range(32):
                e.ldweights(xt(i, t))
                mm = e.matmul(pg.PS[0][:, t * 64:t * 64 + 8], xt(i, t),
                              s_gw[:, i * 8:(i + 1) * 8],
                              start=(i == 0), stop=(i == 31))
            mm.then_inc(sTR, 1)
        for t in range(8):
            e.wait_ge(sVR, 5 * t + 5)
            if t >= 1:
                e.wait_ge(sVR, 40 + t)
            e.transpose(pg.PS[1][0:8, 0:128],
                        s_rout[:, t * 8:(t + 1) * 8],
                        s_idtf[:, :]).then_inc(sTT, 1)
        for t in range(8):
            for i in range(32):
                e.wait_ge(sVR, 5 * 8 + t + 1)
                u = t * 32 + i
                if u >= 2:
                    e.wait_ge(sXS, u - 1)
                e.matmul(pg.PS[2 + u % 2][:, 0:128], s_ics[:, i * 128:(i + 1) * 128],
                         s_rtt[0:8, t * 128:(t + 1) * 128],
                         start=True, stop=True).then_inc(sTS, 1)
        for u in range(64):
            o, t = divmod(u, 8)
            if u >= 2:
                e.wait_ge(sVO, u - 1)
            e.matmul(pg.PS[4 + u % 2][:, :], s_rtt[0:8, t * 128:(t + 1) * 128],
                     s_ocs[:, o * 512:(o + 1) * 512],
                     start=True, stop=True).then_inc(sTO, 1)
        e.wait_ge(sXS, 256)
        e.wait_ge(sVO, 64)
        for hb in range(16):
            o, ih = divmod(hb, 2)
            pW.wait(e, hb)
            for ii in range(16):
                i = ih * 16 + ii
                e.wait_ge(sSG, hb * 16 + ii + 1)
                for t in range(8):
                    if i == 0 and o >= 1:
                        e.wait_ge(sV, (o - 1) * 8 + t + 1)
                    mm = e.matmul(pg.PS[t][:, :], xt(i, t), s_sg[ii % 2][:, :],
                                  start=(i == 0), stop=False)
                mm.then_inc(sTI, 1)
            if ih == 1:
                for t in range(8):
                    mm = e.matmul(pg.PS[t][:, :], s_oner[0:1, 0:128],
                                  s_fb[0:1, o * 512:(o + 1) * 512],
                                  start=False, stop=True)
                    mm.then_inc(sTD, 1)
        e.wait_ge(sTD, 64)

    def evec(e):
        e.memset(s_oner[:, :], 1.0)
        for t in range(8):
            e.wait_ge(sTR, 8)
            e.tensor_copy(s_lg[:, :], pg.PS[0][:, t * 64:t * 64 + 8]).then_inc(sVR, 1)
            e.drain()
            e.reduce_max(st[:, 0:1], s_lg[:, :], axis=AX.X)
            e.drain()
            e.tensor_scalar(st[:, 1:2], st[:, 0:1], -1.0, None,
                            ALU.mult).then_inc(sVR, 1)
            e.wait_ge(sSR, t + 1)
            e.reduce_sum(st[:, 2:3], s_el[:, :], axis=AX.X)
            e.drain()
            e.reciprocal(st[:, 3:4], st[:, 2:3]).then_inc(sVR, 1)
            e.drain()
            e.tensor_scalar(s_rout[:, t * 8:(t + 1) * 8], s_el[:, :],
                            st[:, 3:4], None, ALU.mult).then_inc(sVR, 1)
            e.engine_nop().then_inc(sVR, 1)
        for t in range(8):
            e.wait_ge(sTT, t + 1)
            e.tensor_copy(s_rtt[0:8, t * 128:(t + 1) * 128],
                          pg.PS[1][0:8, 0:128]).then_inc(sVR, 1)
        for u in range(256):
            t, i = divmod(u, 32)
            e.wait_ge(sTS, u + 1)
            e.tensor_tensor(xt(i, t), xt(i, t), pg.PS[2 + u % 2][:, 0:128],
                            ALU.mult).then_inc(sXS, 1)
        for u in range(64):
            e.wait_ge(sTO, u + 1)
            if u >= 2:
                pOSs.wait(e, u - 2)
            e.tensor_copy(s_osb[u % 2][:, :],
                          pg.PS[4 + u % 2][:, :]).then_inc(sVO, 1)
        for u in range(64):
            o, t = divmod(u, 8)
            e.wait_ge(sTD, u + 1)
            pOSl.wait(e, u)
            if u >= 2:
                pFB.wait(e, u - 2)
            e.tensor_tensor(s_fo[u % 2][:, :], pg.PS[t][:, :],
                            s_os[u % 2][:, :], ALU.mult).then_inc(sV, 1)
        e.wait_ge(sV, 64)

    def esca(e):
        for t in range(8):
            e.wait_ge(sVR, 5 * t + 2)
            e.activation(s_el[:, :], s_lg[:, :], AF.Exp,
                         bias=st[:, 1:2]).then_inc(sSR, 1)
        for hb in range(16):
            for ii in range(16):
                u = hb * 16 + ii
                if u >= 2:
                    e.wait_ge(sTI, u - 1)
                pW.wait(e, hb)
                e.sign(s_sg[ii % 2][:, :],
                       s_w[hb % 2][:, ii * 512:(ii + 1) * 512]).then_inc(sSG, 1)
        e.wait_ge(sSG, 256)

    return {"sync": esync, "gpsimd": egps, "tensor": etens,
            "vector": evec, "scalar": esca}



def ph_wqe(pg: Prog):
    wq = pg.dram("wq"); wqed = pg.dram("wqed")
    pW = pg.ping("weW"); pST = pg.ping("weS")
    sP = pg.sem("weP")
    s_wr = [pg.sb(f"wewr{i}", [128, 4096], F32) for i in range(2)]
    s_ta = pg.sb("weta", [128, 8 * 128], F32)
    s_tb = pg.sb("wetb", [128, 8 * 128], F32)
    s_o = [pg.sb(f"weo{i}", [128, 1024], BF16) for i in range(2)]

    def egps(e):
        for d in range(32):
            if d >= 2:
                e.wait_ge(sP, 8 * (d - 1))
            pW.inc(e.dma_start(out=s_wr[d % 2][:, :],
                               in_=wq[d * 128:(d + 1) * 128, :]), d)
        pW.wait_all(e, 32)

    def esync(e):
        for d in range(32):
            e.wait_ge(sP, 8 * (d + 1))
            pST.inc(e.dma_start(out=wqed[d * 128:(d + 1) * 128, :],
                                in_=s_o[d % 2][:, :]), d)
        pST.wait_all(e, 32)

    def evec(e):
        for d in range(32):
            pW.wait(e, d)
            if d >= 1:
                e.wait_ge(sP, 8 * d)
            if d >= 2:
                pST.wait(e, d - 2)
            w = s_wr[d % 2]
            for kv in range(8):
                e.tensor_add(s_ta[:, kv * 128:(kv + 1) * 128],
                             w[:, (4 * kv) * 128:(4 * kv + 1) * 128],
                             w[:, (4 * kv + 1) * 128:(4 * kv + 2) * 128])
                e.tensor_add(s_tb[:, kv * 128:(kv + 1) * 128],
                             w[:, (4 * kv + 2) * 128:(4 * kv + 3) * 128],
                             w[:, (4 * kv + 3) * 128:(4 * kv + 4) * 128])
            e.drain()
            for kv in range(8):
                e.tensor_add(s_o[d % 2][:, kv * 128:(kv + 1) * 128],
                             s_ta[:, kv * 128:(kv + 1) * 128],
                             s_tb[:, kv * 128:(kv + 1) * 128]).then_inc(sP, 1)
        e.wait_ge(sP, 8 * 32)

    return {"gpsimd": egps, "sync": esync, "vector": evec}


def ph_qkv(pg: Prog):
    tln2 = pg.dram("tln2"); wqed = pg.dram("wqed")
    wk = pg.dram("wk"); wv = pg.dram("wv")
    qt = pg.dram("qt"); kcon = pg.dram("kcon"); vcon = pg.dram("vcon")
    bX = pg.batch("qkX"); pQ = pg.ping("qkQ"); pK = pg.ping("qkK")
    pV = pg.ping("qkV"); pSq = pg.ping("qkSq"); pSk = pg.ping("qkSk")
    pSv = pg.ping("qkSv")
    sTq = pg.sem("qkTq"); sTk = pg.sem("qkTk"); sTv = pg.sem("qkTv")
    sVq = pg.sem("qkVq"); sVk = pg.sem("qkVk"); sVv = pg.sem("qkVv")

    s_xt = pg.sb("qkxt", [128, 32 * 1024], BF16)
    s_wq = [pg.sb(f"qkwq{i}", [128, 32 * 128], BF16) for i in range(2)]
    s_wk = [pg.sb(f"qkwk{i}", [128, 32 * 128], BF16) for i in range(2)]
    s_wv = [pg.sb(f"qkwv{i}", [128, 32 * 512], BF16) for i in range(2)]
    s_q = [pg.sb(f"qkq{i}", [128, 1024], BF16) for i in range(2)]
    s_k = [pg.sb(f"qkk{i}", [128, 1024], BF16) for i in range(2)]
    s_v = [pg.sb(f"qkv{i}", [128, 1024], BF16) for i in range(2)]

    def xt(d, sl):
        return s_xt[:, d * 1024 + sl[0]:d * 1024 + sl[1]]

    def esync(e):
        for d in range(32):
            bX.inc(e.dma_start_transpose(
                s_xt[:, d * 1024:(d + 1) * 1024],
                tln2[0:1024, d * 128:(d + 1) * 128]))
        for kv in range(8):
            e.wait_ge(sVq, kv + 1)
            pSq.inc(e.dma_start(out=qt[kv * 128:(kv + 1) * 128, :],
                                in_=s_q[kv % 2][:, :]), kv)
        for kv in range(8):
            e.wait_ge(sVk, kv + 1)
            pSk.inc(e.dma_start(out=kcon[kv * 128:(kv + 1) * 128, :],
                                in_=s_k[kv % 2][:, :]), kv)
        for t in range(8):
            e.wait_ge(sVv, t + 1)
            pSv.inc(e.dma_start(out=vcon[t * 128:(t + 1) * 128, :],
                                in_=s_v[t % 2][:, :]), t)
        pSq.wait_all(e, 8)
        pSk.wait_all(e, 8)
        pSv.wait_all(e, 8)

    def egps(e):
        for kv in range(8):
            if kv >= 2:
                e.wait_ge(sTq, kv - 1)
            pQ.inc(e.dma_start(
                out=s_wq[kv % 2][:, :].rearrange("p (d f) -> p d f", f=128),
                in_=wqed[:, kv * 128:(kv + 1) * 128].rearrange(
                    "(d p) f -> p d f", p=128)), kv)
        for kv in range(8):
            if kv >= 2:
                e.wait_ge(sTk, kv - 1)
            pK.inc(e.dma_start(
                out=s_wk[kv % 2][:, :].rearrange("p (d f) -> p d f", f=128),
                in_=wk[:, kv * 128:(kv + 1) * 128].rearrange(
                    "(d p) f -> p d f", p=128)), kv)
        for oh in range(2):
            pV.inc(e.dma_start(
                out=s_wv[oh % 2][:, :].rearrange("p (d f) -> p d f", f=512),
                in_=wv[:, oh * 512:(oh + 1) * 512].rearrange(
                    "(d p) f -> p d f", p=128)), oh)
        pQ.wait_all(e, 8)
        pK.wait_all(e, 8)
        pV.wait_all(e, 2)

    def etens(e):
        bX.wait_all(e, 32)
        for kv in range(8):
            pQ.wait(e, kv)
            if kv >= 2:
                e.wait_ge(sVq, kv - 1)
            b = (kv % 2) * 2
            for d in range(32):
                wt = s_wq[kv % 2][:, d * 128:(d + 1) * 128]
                e.ldweights(wt)
                e.matmul(pg.PS[b][:, :], wt, xt(d, (0, 512)),
                         start=(d == 0), stop=(d == 31))
                mm = e.matmul(pg.PS[b + 1][:, :], wt, xt(d, (512, 1024)),
                              start=(d == 0), stop=(d == 31))
            mm.then_inc(sTq, 1)
        for kv in range(8):
            pK.wait(e, kv)
            if kv >= 2:
                e.wait_ge(sVk, kv - 1)
            b = 4 + (kv % 2) * 2
            for d in range(32):
                wt = s_wk[kv % 2][:, d * 128:(d + 1) * 128]
                e.ldweights(wt)
                e.matmul(pg.PS[b][:, :], wt, xt(d, (0, 512)),
                         start=(d == 0), stop=(d == 31))
                mm = e.matmul(pg.PS[b + 1][:, :], wt, xt(d, (512, 1024)),
                              start=(d == 0), stop=(d == 31))
            mm.then_inc(sTk, 1)
        e.wait_ge(sVq, 8)
        for t in range(8):
            if t >= 2:
                e.wait_ge(sVv, t - 1)
            b = (t % 2) * 2
            for oh in range(2):
                pV.wait(e, oh)
                for d in range(32):
                    wt = s_xt[:, d * 1024 + t * 128:d * 1024 + (t + 1) * 128]
                    e.ldweights(wt)
                    mm = e.matmul(pg.PS[b + oh][:, :], wt,
                                  s_wv[oh % 2][:, d * 512:(d + 1) * 512],
                                  start=(d == 0), stop=(d == 31))
            mm.then_inc(sTv, 1)
        e.wait_ge(sTv, 8)

    def evec(e):
        for kv in range(8):
            e.wait_ge(sTq, kv + 1)
            if kv >= 2:
                pSq.wait(e, kv - 2)
            b = (kv % 2) * 2
            e.tensor_scalar(s_q[kv % 2][:, 0:512], pg.PS[b][:, :],
                            1.0 / (128.0 ** 0.5), None, ALU.mult)
            e.tensor_scalar(s_q[kv % 2][:, 512:1024], pg.PS[b + 1][:, :],
                            1.0 / (128.0 ** 0.5), None, ALU.mult).then_inc(sVq, 1)
        for kv in range(8):
            e.wait_ge(sTk, kv + 1)
            if kv >= 2:
                pSk.wait(e, kv - 2)
            b = 4 + (kv % 2) * 2
            e.tensor_copy(s_k[kv % 2][:, 0:512], pg.PS[b][:, :])
            e.tensor_copy(s_k[kv % 2][:, 512:1024],
                          pg.PS[b + 1][:, :]).then_inc(sVk, 1)
        for t in range(8):
            e.wait_ge(sTv, t + 1)
            if t >= 2:
                pSv.wait(e, t - 2)
            b = (t % 2) * 2
            e.tensor_copy(s_v[t % 2][:, 0:512], pg.PS[b][:, :])
            e.tensor_copy(s_v[t % 2][:, 512:1024],
                          pg.PS[b + 1][:, :]).then_inc(sVv, 1)
        e.wait_ge(sVv, 8)

    return {"sync": esync, "gpsimd": egps, "tensor": etens, "vector": evec}



def ph_attn(pg: Prog):
    qt = pg.dram("qt"); kcon = pg.dram("kcon"); vcon = pg.dram("vcon")
    kconI = pg.dram("kconI"); vconI = pg.dram("vconI")
    kg = pg.dram("kg"); vg = pg.dram("vg")
    wo = pg.dram("wo"); atto = pg.dram("atto")

    bBB = pg.batch("atBB"); bQT = pg.batch("atQT")
    pKT = pg.ping("atKT", per=4); pVG = pg.ping("atVG")
    pWO = pg.ping("atWO"); pAO = pg.ping("atAO")
    sCC = pg.sem("atCC")
    sQS = pg.sem("atQS"); sKS = pg.sem("atKS")
    sKQ = pg.sem("atKQ"); sKE = pg.sem("atKE")
    sUQ = pg.sem("atUQ"); sUE = pg.sem("atUE")
    sNU = pg.sem("atNU")
    sTS2 = pg.sem("atT2"); sEX = pg.sem("atEX")
    sRS1 = pg.sem("atR1"); sVRS = pg.sem("atVR")
    sBC2 = pg.sem("atB2"); sVS = pg.sem("atVS"); sPV = pg.sem("atPV")
    sTO = pg.sem("atTO"); sVO = pg.sem("atVO"); sMX = pg.sem("atMX")
    sIN = pg.sem("atIN")

    s_qt = pg.sb("atqt", [128, 8 * 1024], BF16)
    s_kt = [pg.sb(f"atkt{i}", [128, 4096], BF16) for i in range(2)]
    s_vgh = [pg.sb(f"atvg{i}", [128, 32 * 128], BF16) for i in range(2)]
    s_pt = pg.sb("atpt", [128, 32 * 1024], BF16)
    s_vtt = pg.sb("atvtt", [128, 8 * 1024], BF16)
    s_vf = pg.sb("atvf", [128, 512], F32)
    s_qsq = pg.sb("atqsq", [128, 1024], F32)
    s_ksq = pg.sb("atksq", [128, 4096], F32)
    s_ksr = pg.sb("atksr", [1, 4096], F32)
    s_usr = pg.sb("atusr", [1, 1024], F32)
    s_rr = pg.sb("atrr", [1, 512], F32)
    s_rc = pg.sb("atrc", [1, 512], F32)
    s_negu = pg.sb("atnegu", [1, 512], BF16)
    s_onesf = pg.sb("atof", [128, 1], F32)
    s_onesfr = pg.sb("atofr", [1, 128], F32)
    s_onesb = pg.sb("atob", [128, 1], BF16)
    s_onerb = pg.sb("atorb", [1, 512], BF16)
    s_wo = [pg.sb(f"atwo{i}", [128, 8 * 512], BF16) for i in range(2)]
    s_ao = [pg.sb(f"atao{i}", [128, 512], BF16) for i in range(2)]
    stt = pg.sb("atst", [1, 16], F32)

    def esync(e):
        bBB.inc(e.dma_start(out=kconI, in_=kcon))
        bBB.inc(e.dma_start(out=vconI, in_=vcon))
        for kv in range(8):
            bQT.inc(e.dma_start(out=s_qt[:, kv * 1024:(kv + 1) * 1024],
                                in_=qt[kv * 128:(kv + 1) * 128, :]))
        for h in range(8):
            e.wait_ge(sCC, 2)
            if h >= 2:
                e.wait_ge(sTS2, (h - 1) * 64)
            for r in range(4):
                pKT.inc(e.dma_start(
                    out=s_kt[h % 2][:, r * 1024:(r + 1) * 1024],
                    in_=kg[r * 1024 + h * 128:r * 1024 + (h + 1) * 128, :]), h)
            if h >= 2:
                e.wait_ge(sPV, 2 * (h - 1))
            pVG.inc(e.dma_start(
                out=s_vgh[h % 2][:, :].rearrange("p (kc f) -> p kc f", f=128),
                in_=vg[:, h * 128:(h + 1) * 128].rearrange(
                    "(kc p) f -> p kc f", p=128)), h)
        for u in range(64):
            o, t = divmod(u, 8)
            e.wait_ge(sVO, u + 1)
            pAO.inc(e.dma_start(
                out=atto[t * 128:(t + 1) * 128, o * 512:(o + 1) * 512],
                in_=s_ao[u % 2][:, :]), u)
        pAO.wait_all(e, 64)

    def egps(e):
        bBB.wait_all(e, 2)
        e.collective_compute(
            "AllGather", ALU.bypass,
            replica_groups=[[0, 1, 2, 3], [4, 5, 6, 7]],
            ins=[kconI.opt()], outs=[kg.opt()]).then_inc(sCC)
        e.collective_compute(
            "AllGather", ALU.bypass,
            replica_groups=[[0, 1, 2, 3], [4, 5, 6, 7]],
            ins=[vconI.opt()], outs=[vg.opt()]).then_inc(sCC)
        for o in range(8):
            if o >= 2:
                e.wait_ge(sTO, 8 * (o - 1))
            pWO.inc(e.dma_start(
                out=s_wo[o % 2][:, :].rearrange("p (hc f) -> p hc f", f=512),
                in_=wo[:, o * 512:(o + 1) * 512].rearrange(
                    "(hc p) f -> p hc f", p=128)), o)
        pWO.wait_all(e, 8)
        e.wait_ge(sCC, 2)

    def etens(e):
        e.wait_ge(sIN, 1)
        for h in range(8):
            e.wait_ge(sQS, h + 1)
            if h >= 1:
                e.wait_ge(sVS, 2 * h)
            for qh in range(2):
                if h + qh > 0:
                    e.wait_ge(sUE, 2 * h + qh)
                e.matmul(pg.PS[6][0:1, 0:512], s_onesf[:, :],
                         s_qsq[:, qh * 512:(qh + 1) * 512],
                         start=True, stop=True).then_inc(sUQ, 1)
            e.wait_ge(sKS, h + 1)
            for j in range(8):
                if h + j > 0:
                    e.wait_ge(sKE, 8 * h + j)
                e.matmul(pg.PS[7][0:1, 0:512], s_onesf[:, :],
                         s_ksq[:, j * 512:(j + 1) * 512],
                         start=True, stop=True).then_inc(sKQ, 1)
            e.wait_ge(sNU, h + 1)
            pKT.wait(e, h)
            for kc in range(32):
                r, tl = divmod(kc, 8)
                for qh in range(2):
                    idx = h * 64 + kc * 2 + qh
                    if idx >= 2:
                        e.wait_ge(sEX, idx - 1)
                    kt = s_kt[h % 2][:, r * 1024 + tl * 128:
                                     r * 1024 + (tl + 1) * 128]
                    e.ldweights(kt)
                    e.matmul(pg.PS[idx % 2][:, :], kt,
                             s_qt[:, h * 1024 + qh * 512:
                                  h * 1024 + (qh + 1) * 512],
                             start=True, stop=False)
                    e.matmul(pg.PS[idx % 2][:, :], s_onerb[0:1, 0:128],
                             s_negu[0:1, :],
                             start=False, stop=True).then_inc(sTS2, 1)
            e.wait_ge(sEX, (h + 1) * 64)
            for qh in range(2):
                if qh == 1:
                    e.wait_ge(sVRS, 2 * h + 1)
                elif h >= 2:
                    e.wait_ge(sVRS, 2 * (h - 1))
                for kc in range(32):
                    e.ldweights(s_onesb[:, :])
                    mm = e.matmul(
                        pg.PS[4 + h % 2][0:1, 0:512], s_onesb[:, :],
                        s_pt[:, kc * 1024 + qh * 512:kc * 1024 + (qh + 1) * 512],
                        start=(kc == 0), stop=(kc == 31))
                mm.then_inc(sRS1, 1)
            pVG.wait(e, h)
            for qh in range(2):
                if h >= 1:
                    e.wait_ge(sVS, 2 * h)
                for kc in range(32):
                    vt = s_vgh[h % 2][:, kc * 128:(kc + 1) * 128]
                    e.ldweights(vt)
                    mm = e.matmul(
                        pg.PS[2 + qh][:, :], vt,
                        s_pt[:, kc * 1024 + qh * 512:kc * 1024 + (qh + 1) * 512],
                        start=(kc == 0), stop=(kc == 31))
                mm.then_inc(sPV, 1)
            for qh in range(2):
                e.wait_ge(sVRS, 2 * h + qh + 1)
                if 2 * h + qh >= 1:
                    e.wait_ge(sVS, 2 * h + qh)
                e.matmul(pg.PS[6][:, 0:512], s_onesfr[0:1, :], s_rc[0:1, :],
                         start=True, stop=True).then_inc(sBC2, 1)
        e.wait_ge(sVS, 16)
        for o in range(8):
            pWO.wait(e, o)
            for t in range(8):
                u = o * 8 + t
                if u >= 2:
                    e.wait_ge(sVO, u - 1)
                for hc in range(8):
                    vt = s_vtt[:, hc * 1024 + t * 128:hc * 1024 + (t + 1) * 128]
                    e.ldweights(vt)
                    mm = e.matmul(pg.PS[u % 2][:, :], vt,
                                  s_wo[o % 2][:, hc * 512:(hc + 1) * 512],
                                  start=(hc == 0), stop=(hc == 7))
                mm.then_inc(sTO, 1)
        e.wait_ge(sTO, 64)

    def evec(e):
        e.memset(s_onesf[:, :], 1.0)
        e.memset(s_onerb[:, :], 1.0)
        e.memset(s_onesfr[:, :], 1.0)
        e.drain()
        e.memset(s_onesb[:, :], 1.0).then_inc(sIN, 1)
        for h in range(8):
            for qh in range(2):
                e.wait_ge(sUQ, 2 * h + qh + 1)
                e.tensor_copy(s_usr[0:1, qh * 512:(qh + 1) * 512],
                              pg.PS[6][0:1, 0:512]).then_inc(sUE, 1)
            for j in range(8):
                e.wait_ge(sKQ, 8 * h + j + 1)
                e.tensor_copy(s_ksr[0:1, j * 512:(j + 1) * 512],
                              pg.PS[7][0:1, 0:512]).then_inc(sKE, 1)
            e.drain()
            e.reduce_max(stt[0:1, 0:1], s_usr[0:1, :], axis=AX.X)
            e.reduce_max(stt[0:1, 1:2], s_ksr[0:1, :], axis=AX.X)
            e.drain()
            e.tensor_tensor(stt[0:1, 2:3], stt[0:1, 0:1], stt[0:1, 1:2],
                            ALU.mult).then_inc(sMX, 1)
            for qh in range(2):
                if qh == 1:
                    e.wait_ge(sBC2, 2 * h + 1)
                e.wait_ge(sRS1, 2 * h + qh + 1)
                e.tensor_copy(s_rr[0:1, :],
                              pg.PS[4 + h % 2][0:1, 0:512])
                e.drain()
                e.reciprocal(s_rc[0:1, :], s_rr[0:1, :]).then_inc(sVRS, 1)
                e.wait_ge(sPV, 2 * h + qh + 1)
                e.tensor_copy(s_vf[:, :], pg.PS[2 + qh][:, :])
                e.wait_ge(sBC2, 2 * h + qh + 1)
                e.drain()
                e.tensor_tensor(
                    s_vtt[:, h * 1024 + qh * 512:h * 1024 + (qh + 1) * 512],
                    s_vf[:, :], pg.PS[6][:, 0:512], ALU.mult).then_inc(sVS, 1)
        for u in range(64):
            e.wait_ge(sTO, u + 1)
            if u >= 2:
                pAO.wait(e, u - 2)
            e.tensor_copy(s_ao[u % 2][:, :],
                          pg.PS[u % 2][:, :]).then_inc(sVO, 1)
        e.wait_ge(sVO, 64)

    def esca(e):
        bQT.wait_all(e, 8)
        e.wait_ge(sIN, 1)
        for h in range(8):
            if h >= 1:
                e.wait_ge(sTS2, h * 64)
            e.activation(s_qsq[:, :], s_qt[:, h * 1024:(h + 1) * 1024],
                         AF.Square).then_inc(sQS, 1)
            pKT.wait(e, h)
            if h >= 1:
                e.wait_ge(sKQ, 8 * h)
            e.activation(s_ksq[:, :], s_kt[h % 2][:, :],
                         AF.Square).then_inc(sKS, 1)
            e.wait_ge(sMX, h + 1)
            e.activation(stt[0:1, 3:4], stt[0:1, 2:3], AF.Sqrt)
            e.drain()
            e.activation(s_negu[0:1, :], s_onerb[0:1, :], AF.Identity,
                         scale=stt[0:1, 3:4])
            e.drain()
            e.activation(s_negu[0:1, :], s_negu[0:1, :], AF.Identity,
                         scale=-1.0).then_inc(sNU, 1)
            for kc in range(32):
                for qh in range(2):
                    idx = h * 64 + kc * 2 + qh
                    e.wait_ge(sTS2, idx + 1)
                    e.activation(
                        s_pt[:, kc * 1024 + qh * 512:kc * 1024 + (qh + 1) * 512],
                        pg.PS[idx % 2][:, :], AF.Exp).then_inc(sEX, 1)
        e.wait_ge(sEX, 512)

    return {"sync": esync, "gpsimd": egps, "tensor": etens,
            "vector": evec, "scalar": esca}


def ph_ffn1(pg: Prog):
    tln3 = pg.dram("tln3"); bwT = pg.dram("bwT"); bb = pg.dram("bb")
    att = pg.dram("att"); bt2 = pg.dram("bt2"); rwT = pg.dram("rwT")
    idtf = pg.dram("idtf"); fbuf = pg.dram("fbuf")

    bX = pg.batch("f1X"); bC = pg.batch("f1C")
    pW = pg.ping("f1W"); pB2 = pg.ping("f1B2"); pFB = pg.ping("f1FB")
    sTR = pg.sem("f1TR"); sT1 = pg.sem("f1T1"); sVR = pg.sem("f1VR")
    sSR = pg.sem("f1SR"); sCB = pg.sem("f1CB"); sRT = pg.sem("f1RT")
    sHB = pg.sem("f1HB"); sTH = pg.sem("f1TH"); sHE = pg.sem("f1HE")
    sTI = pg.sem("f1TI"); sTD = pg.sem("f1TD"); sV = pg.sem("f1V")

    s_xt = pg.sb("f1xt", [128, 32 * 1024], BF16)
    s_rw = pg.sb("f1rw", [128, 32 * 16], BF16)
    s_at = pg.sb("f1at", [128, 32 * 64], BF16)
    s_bb = pg.sb("f1bb", [1, 4096], BF16)
    s_on1 = pg.sb("f1on", [1, 128], BF16)
    s_idtf = pg.sb("f1id", [128, 128], F32)
    s_lg = pg.sb("f1lg", [128, 16], F32)
    s_el = pg.sb("f1el", [128, 16], F32)
    s_mk = pg.sb("f1mk", [128, 16], F32)
    s_sel = pg.sb("f1sl", [128, 16], F32)
    s_comb = pg.sb("f1cb", [128, 128], F32)
    st = pg.sb("f1st", [128, 8], F32)
    s_rht = pg.sb("f1rh", [128, 512], F32)
    s_H = pg.sb("f1H", [128, 8 * 1024], F32)
    s_htT = pg.sb("f1hT", [128, 8 * 1024], BF16)
    s_w = [pg.sb(f"f1w{i}", [128, 16 * 512], BF16) for i in range(2)]
    s_b2 = [pg.sb(f"f1b2{i}", [128, 8 * 512], BF16) for i in range(2)]
    s_fo = [pg.sb(f"f1fo{i}", [128, 512], BF16) for i in range(2)]

    def xt(i, t):
        return s_xt[:, i * 1024 + t * 128:i * 1024 + (t + 1) * 128]

    def esync(e):
        for d in range(32):
            bX.inc(e.dma_start_transpose(
                s_xt[:, d * 1024:(d + 1) * 1024],
                tln3[0:1024, d * 128:(d + 1) * 128]))
        for u in range(64):
            o, t = divmod(u, 8)
            e.wait_ge(sV, u + 1)
            pFB.inc(e.dma_start(
                out=fbuf[t * 128:(t + 1) * 128, o * 512:(o + 1) * 512],
                in_=s_fo[u % 2][:, :]), u)
        pFB.wait_all(e, 64)

    def egps(e):
        bC.inc(e.dma_start(out=s_rw[:, :].rearrange("p (i o) -> p i o", o=16),
                           in_=rwT.rearrange("(i p) o -> p i o", p=128)))
        bC.inc(e.dma_start(out=s_at[:, :].rearrange("p (i o) -> p i o", o=64),
                           in_=att.rearrange("(i p) o -> p i o", p=128)))
        bC.inc(e.dma_start(out=s_bb[:, :], in_=bb))
        bC.inc(e.dma_start(out=s_idtf[:, :], in_=idtf))
        for hb in range(16):
            o, ih = divmod(hb, 2)
            if hb >= 2:
                e.wait_ge(sTI, (hb - 1) * 16)
            pW.inc(e.dma_start(
                out=s_w[hb % 2][:, :].rearrange("p (i f) -> p i f", f=512),
                in_=bwT[ih * 2048:(ih + 1) * 2048,
                        o * 512:(o + 1) * 512].rearrange(
                    "(i p) f -> p i f", p=128)), hb)
            if ih == 1:
                if o >= 2:
                    e.wait_ge(sTD, (o - 1) * 8)
                pB2.inc(e.dma_start(
                    out=s_b2[o % 2][:, :].rearrange("p (i f) -> p i f", f=512),
                    in_=bt2[:, o * 512:(o + 1) * 512].rearrange(
                        "(i p) f -> p i f", p=128)), o)
        pW.wait_all(e, 16)
        pB2.wait_all(e, 8)

    def etens(e):
        bX.wait_all(e, 32)
        bC.wait_all(e, 4)
        for t in range(8):
            for i in range(32):
                mm = e.matmul(pg.PS[0][:, t * 64:t * 64 + 16], xt(i, t),
                              s_rw[:, i * 16:(i + 1) * 16],
                              start=(i == 0), stop=(i == 31))
            mm.then_inc(sTR, 1)
        for t in range(8):
            for i in range(32):
                mm = e.matmul(pg.PS[3][:, t * 64:(t + 1) * 64], xt(i, t),
                              s_at[:, i * 64:(i + 1) * 64],
                              start=(i == 0), stop=(i == 31))
            mm.then_inc(sT1, 1)
        for g in range(16):
            b = 4 + g % 2
            if g >= 2:
                e.wait_ge(sHE, 4 * (g - 1))
            for j in range(4):
                u = 4 * g + j
                t, i2 = divmod(u, 8)
                e.wait_ge(sHB, t + 1)
                e.transpose(pg.PS[b][:, j * 128:(j + 1) * 128],
                            s_H[:, t * 1024 + i2 * 128:
                                t * 1024 + (i2 + 1) * 128],
                            s_idtf[:, :]).then_inc(sTH, 1)
        e.wait_ge(sHE, 64)
        e.wait_ge(sCB, 8)
        e.wait_ge(sRT, 8)
        for hb in range(16):
            o, ih = divmod(hb, 2)
            pW.wait(e, hb)
            for ii in range(16):
                i = ih * 16 + ii
                for t in range(8):
                    if i == 0 and o >= 1:
                        e.wait_ge(sV, (o - 1) * 8 + t + 1)
                    mm = e.matmul(pg.PS[t][:, :], xt(i, t),
                                  s_w[hb % 2][:, ii * 512:(ii + 1) * 512],
                                  start=(i == 0), stop=False)
                mm.then_inc(sTI, 1)
            if ih == 1:
                pB2.wait(e, o)
                for t in range(8):
                    for i2 in range(8):
                        e.matmul(pg.PS[t][:, :],
                                 s_htT[:, i2 * 1024 + t * 128:
                                       i2 * 1024 + (t + 1) * 128],
                                 s_b2[o % 2][:, i2 * 512:(i2 + 1) * 512],
                                 start=False, stop=False)
                    mm = e.matmul(pg.PS[t][:, :], s_on1[0:1, 0:128],
                                  s_bb[0:1, o * 512:(o + 1) * 512],
                                  start=False, stop=True)
                    mm.then_inc(sTD, 1)
        e.wait_ge(sTD, 64)

    def evec(e):
        e.memset(s_on1[:, :], 1.0)
        for t in range(8):
            e.wait_ge(sTR, 8)
            if t >= 1:
                e.wait_ge(sSR, t)
            e.tensor_copy(s_lg[:, :],
                          pg.PS[0][:, t * 64:t * 64 + 16]).then_inc(sVR, 1)
            e.drain()
            e.reduce_max(st[:, 0:1], s_lg[:, :], axis=AX.X)
            e.drain()
            e.tensor_scalar(st[:, 1:2], st[:, 0:1], -1.0, None,
                            ALU.mult).then_inc(sVR, 1)
            e.wait_ge(sSR, t + 1)
            e.reduce_sum(st[:, 2:3], s_el[:, :], axis=AX.X)
            e.reduce_max(st[:, 7:8], s_el[:, :], axis=AX.X)
            e.drain()
            e.tensor_scalar(s_mk[:, :], s_el[:, :], st[:, 7:8], None,
                            ALU.is_equal)
            e.tensor_scalar(st[:, 3:4], st[:, 2:3], 2e-6, None, ALU.mult)
            e.drain()
            e.scalar_tensor_tensor(s_mk[:, :], s_mk[:, :], -2.0, s_el[:, :],
                                   ALU.mult, ALU.add)
            e.drain()
            e.reduce_max(st[:, 4:5], s_mk[:, :], axis=AX.X)
            e.drain()
            e.scalar_tensor_tensor(st[:, 5:6], st[:, 4:5], 2.0, st[:, 3:4],
                                   ALU.mult, ALU.add)
            e.drain()
            e.scalar_tensor_tensor(st[:, 5:6], st[:, 7:8], 2.0, st[:, 5:6],
                                   ALU.mult, ALU.add)
            e.drain()
            e.reciprocal(st[:, 6:7], st[:, 5:6])
            e.tensor_scalar(s_sel[:, :], s_el[:, :], st[:, 4:5], None,
                            ALU.is_ge)
            e.drain()
            e.scalar_tensor_tensor(s_comb[:, t * 16:(t + 1) * 16], s_el[:, :],
                                   st[:, 6:7], s_sel[:, :],
                                   ALU.mult, ALU.mult).then_inc(sCB, 1)
        e.wait_ge(sT1, 8)
        for t in range(8):
            e.tensor_copy(s_rht[:, t * 64:(t + 1) * 64],
                          pg.PS[3][:, t * 64:(t + 1) * 64]).then_inc(sRT, 1)
        for t in range(8):
            e.wait_ge(sCB, t + 1)
            e.wait_ge(sRT, t + 1)
            for ex in range(16):
                ins = e.tensor_scalar(
                    s_H[:, t * 1024 + ex * 64:t * 1024 + (ex + 1) * 64],
                    s_rht[:, t * 64:(t + 1) * 64],
                    s_comb[:, t * 16 + ex:t * 16 + ex + 1], None, ALU.mult)
            ins.then_inc(sHB, 1)
        for u in range(64):
            t, i2 = divmod(u, 8)
            g, j = divmod(u, 4)
            e.wait_ge(sTH, 4 * (g + 1))
            b = 4 + g % 2
            e.tensor_copy(s_htT[:, i2 * 1024 + t * 128:
                                i2 * 1024 + (t + 1) * 128],
                          pg.PS[b][:, j * 128:(j + 1) * 128]
                          ).then_inc(sHE, 1)
        for u in range(64):
            o, t = divmod(u, 8)
            e.wait_ge(sTD, u + 1)
            if u >= 2:
                pFB.wait(e, u - 2)
            e.tensor_copy(s_fo[u % 2][:, :], pg.PS[t][:, :]).then_inc(sV, 1)
        e.wait_ge(sV, 64)

    def esca(e):
        for t in range(8):
            e.wait_ge(sVR, 2 * t + 2)
            e.activation(s_el[:, :], s_lg[:, :], AF.Exp,
                         bias=st[:, 1:2]).then_inc(sSR, 1)
        e.wait_ge(sSR, 8)

    return {"sync": esync, "gpsimd": egps, "tensor": etens,
            "vector": evec, "scalar": esca}


PHASE_TABLE = {
    "ln0": ph_ln0,
    "inproj": ph_inproj,
    "ssd1": ph_ssd1,
    "ssd2": ph_ssd2,
    "outproj": ph_outproj,
    "bound1": make_bound("b1", "abuf", "x", "res1", "tln1"),
    "bound2": make_bound("b2", "fbuf", "res1", "res2", "tln2"),
    "bound3": make_bound("b3", "atto", "res2", "res3", "tln3"),
    "bound4": make_bound("b4", "fbuf", "res3", "out", "out", final=True),
    "ffn0": ph_ffn0,
    "wqe": ph_wqe,
    "qkv": ph_qkv,
    "attn": ph_attn,
    "ffn1": ph_ffn1,
}

PHASE_READS = {
    "ln0": ["x", "xh"],
    "inproj": ["tln0", "tlnh", "wprojT", "cw", "cb", "dtb"],
    "ssd1": ["hsfm", "bcfm", "dtac", "alog", "coefc", "idtf"],
    "ssd2": ["hsfm", "bcfm", "gatefm", "saux", "acsr", "sleft", "maskneg",
             "mask01", "ddr"],
    "outproj": ["norm", "w2T", "nw"],
    "bound1": ["abuf", "x"],
    "bound2": ["fbuf", "res1"],
    "bound3": ["atto", "res2"],
    "bound4": ["fbuf", "res3"],
    "ffn0": ["tln1", "fwT", "fbias", "gwT", "ics", "ocs", "idtf"],
    "wqe": ["wq"],
    "qkv": ["tln2", "wqed", "wk", "wv"],
    "attn": ["qt", "kcon", "vcon", "wo"],
    "ffn1": ["tln3", "bwT", "bb", "att", "bt2", "rwT", "idtf"],
}
PHASE_WRITES = {
    "ln0": ["tln0", "tlnh"],
    "inproj": ["gatefm", "hsfm", "bcfm", "dtac"],
    "ssd1": ["saux", "acsr", "sleft"],
    "ssd2": ["norm"],
    "outproj": ["abuf"],
    "bound1": ["res1", "tln1"],
    "bound2": ["res2", "tln2"],
    "bound3": ["res3", "tln3"],
    "bound4": ["out"],
    "ffn0": ["fbuf"],
    "wqe": ["wqed"],
    "qkv": ["qt", "kcon", "vcon"],
    "attn": ["atto"],
    "ffn1": ["fbuf"],
}

LAUNCH_GROUPS = [["ln0", "inproj"], ["ssd1"], ["ssd2"],
                 ["outproj", "bound1"], ["ffn0", "bound2"], ["wqe", "qkv"],
                 ["attn", "bound3"], ["ffn1", "bound4"]]


def launch_plan(groups):
    plans = []
    for gi, group in enumerate(groups):
        produced = set()
        needs = set()
        for p in group:
            needs |= set(PHASE_READS[p]) - produced
            produced |= set(PHASE_WRITES[p])
        later_reads = set()
        for g2 in groups[gi + 1:]:
            for p in g2:
                later_reads |= set(PHASE_READS[p])
        outs = {w for w in produced if w in later_reads or w == "out"}
        plans.append((sorted(needs), sorted(outs)))
    return plans


def stage_inputs(inputs):
    x0 = np.asarray(inputs["L0_attn_hidden_states"], dtype=np.float32)
    asf = lambda a: np.ascontiguousarray(np.asarray(a), dtype=np.float32)
    w = {
        "wprojT": asf(np.asarray(inputs["L0_attn_in_proj_w"]).T),
        "cw": asf(np.asarray(inputs["L0_attn_conv_w"]).reshape(80, 128, 4)
                  .transpose(1, 0, 2).reshape(128, 320)),
        "cb": asf(np.asarray(inputs["L0_attn_conv_b"]).reshape(80, 128).T),
        "dtb": asf(np.asarray(inputs["L0_attn_dt_bias"]).reshape(H, 1)),
        "alog": asf(np.asarray(inputs["L0_attn_A_log"]).reshape(H, 1)),
        "dd": asf(np.asarray(inputs["L0_attn_D"]).reshape(H, 1)),
        "nw": asf(np.asarray(inputs["L0_attn_norm_weight"]).reshape(64, 128).T),
        "w2T": asf(np.asarray(inputs["L0_attn_out_proj_w"]).T),
        "fwT": asf(np.asarray(inputs["L0_ffn_weight"]).T),
        "fbias": asf(np.asarray(inputs["L0_ffn_bias"]).reshape(1, D)),
        "gwT": asf(np.asarray(inputs["L0_ffn_gate_w"]).T),
        "ics": asf(inputs["L0_ffn_in_channel_scale"]),
        "ocs": asf(inputs["L0_ffn_out_channel_scale"]),
        "wq": asf(inputs["L1_attn_wq"]),
        "wk": asf(inputs["L1_attn_wk"]),
        "wv": asf(inputs["L1_attn_wv"]),
        "wo": asf(inputs["L1_attn_wo"]),
        "bwT": asf(np.asarray(inputs["L1_ffn_base_w"]).T),
        "bb": asf(np.asarray(inputs["L1_ffn_base_b"]).reshape(1, D)),
        "att": asf(np.asarray(inputs["L1_ffn_A"]).T),
        "bt2": asf(np.asarray(inputs["L1_ffn_B"]).transpose(0, 2, 1).reshape(16 * 64, D)),
        "rwT": asf(np.asarray(inputs["L1_ffn_router_w"]).T),
    }
    idt = np.eye(128, dtype=bfloat16)
    idtf = np.eye(128, dtype=np.float32)
    oner = np.ones((1, 512), dtype=bfloat16)
    mask01 = np.ascontiguousarray(np.triu(np.ones((128, 128), np.float32)))
    maskneg = np.ascontiguousarray(
        np.where(mask01 > 0, 0.0, -1e30).astype(np.float32))
    ddrow = asf(np.asarray(inputs["L0_attn_D"]).reshape(1, H))
    per_core = []
    for c in range(NCORES):
        b, q = divmod(c, 4)
        lo = q * T
        xm = np.ascontiguousarray(x0[b, lo:lo + T, :])
        if q == 0:
            xhs = np.zeros((3, D), np.float32)
        else:
            xhs = np.ascontiguousarray(x0[b, lo - 3:lo, :])
        coefc = np.zeros((128, 40), np.float32)
        for r in range(4):
            for m in range(4):
                emul = 1.0 if (r < m < q) else 0.0
                coefc[:, r * 4 + m] = emul
                coefc[:, 24 + r * 4 + m] = 1.0 - emul
            coefc[:, 16 + r] = 1.0 if r < q else 0.0
        per_core.append(dict(x=xm, xh=xhs, idt=idt, idtf=idtf, oner=oner,
                             mask01=mask01, maskneg=maskneg, ddr=ddrow,
                             coefc=coefc, **w))
    return per_core


def run_groups(board, groups):
    for (needs, outs), group in zip(launch_plan(groups), groups):
        pg = Prog(ext_in=needs, ext_out=outs)
        for n in group:
            pg.arena_reset()
            pg.add_phase(PHASE_TABLE[n](pg))
        nc = pg.finalize()
        in_maps = [{n: board[c][n] for n in needs} for c in range(NCORES)]
        res = run_bass_kernel_spmd(nc, in_maps, core_ids=list(range(NCORES)))
        for c in range(NCORES):
            for n in outs:
                board[c][n] = np.asarray(res.results[c][n])
    return board


def kernel(**inputs):
    board = stage_inputs(inputs)
    run_groups(board, LAUNCH_GROUPS)
    out = np.empty((B, L, D), np.float32)
    for c in range(NCORES):
        b, q = divmod(c, 4)
        out[b, q * T:(q + 1) * T, :] = board[c]["out"]
    return out
